# version 46
# baseline (speedup 1.0000x reference)
import math
from contextlib import ExitStack

import numpy as np
import ml_dtypes

import concourse.bass as bass
import concourse.mybir as mybir
from concourse.bass_utils import run_bass_kernel_spmd

F32 = mybir.dt.float32
BF16 = mybir.dt.bfloat16
AF = mybir.ActivationFunctionType
ALU = mybir.AluOpType
AX = mybir.AxisListType
BFNP = ml_dtypes.bfloat16

D = 1024
NCH = 8
NE = 16
MID_R = 256
MID_S = 512
NSH = 2
SCALE = 1.0 / 32.0
EPS_LN = 1e-5
EPS_RT = 1e-9

ENGS = ("T", "V", "S", "Y", "G")
NYS, NAS, NGS = 8, 4, 4
SEMS = (("T", "V", "S")
        + tuple("dY%d" % i for i in range(NYS))
        + tuple("dA%d" % i for i in range(NAS))
        + tuple("dG%d" % i for i in range(NGS)))


class Plan:

    def __init__(self):
        self.ops = {e: [] for e in ENGS}
        self.count = {s: 0 for s in SEMS}
        self.waited = {}
        self.pos = {e: 0 for e in ENGS}
        self.drained = {e: 0 for e in ENGS}

    def op(self, eng, meth, *args, waits=(), inc=None, inc_amt=1,
           dma=False, **kw):
        w = []
        drain = False
        for tok in waits:
            if tok is None:
                continue
            if tok[0] == "@":
                _, teng, tpos, stok = tok
                if teng == eng:
                    if eng == "T" or (eng == "S" and not dma):
                        continue
                    if stok is not None:
                        tok = stok
                    else:
                        if tpos >= self.drained[eng]:
                            drain = True
                            self.drained[eng] = self.pos[eng]
                        continue
                tok = stok
                if tok is None:
                    raise RuntimeError(
                        "cross-engine dep %s->%s without a semaphore token"
                        % (teng, eng))
            s, v = tok
            if v <= 0:
                continue
            key = (eng, s)
            if self.waited.get(key, 0) >= v:
                continue
            self.waited[key] = v
            w.append((s, v))
        incs = []
        tok = None
        if inc is not None:
            self.count[inc] += inc_amt
            tok = (inc, self.count[inc])
            incs.append((inc, inc_amt))
        self.ops[eng].append((meth, args, kw, tuple(w), tuple(incs), drain))
        self.pos[eng] += 1
        return tok

    def tok(self, s):
        return (s, self.count[s])

    def all_toks(self):
        return [(s, self.count[s]) for s in SEMS]

    def replay(self, eng_name, eng, sems):
        for meth, args, kw, waits, incs, drain in self.ops[eng_name]:
            if drain:
                eng.drain()
            for s, v in waits:
                eng.wait_ge(sems[s], v)
            if meth == "_wait_only":
                continue
            inst = getattr(eng, meth)(*args, **kw)
            for s, amt in incs:
                inst = inst.then_inc(sems[s], amt)


class Buf:

    def __init__(self):
        self.w = []
        self.r = []
        self.gen_waits = []

    def begin(self):
        self.gen_waits = self.w + self.r
        self.w = []
        self.r = []
        return list(self.gen_waits)

    def wrote(self, tok):
        if tok is not None:
            self.w.append(tok)

    def rd(self):
        return list(self.w)

    def read(self, tok):
        if tok is not None:
            self.r.append(tok)


class Builder:
    def __init__(self, S):
        assert S % 1024 == 0
        self.S = S
        self.NTT = S // 512
        self.NKC = S // 128
        self.NKB = S // 1024
        nc = bass.Bass("TRN2", target_bir_lowering=False, debug=False)
        self.nc = nc
        self.P = Plan()
        self.root = ExitStack()
        self.sp = None
        self.bufs = {}

        dr = lambda n, sh, dt, k=None: (
            nc.dram_tensor(n, sh, dt, kind=k) if k else nc.dram_tensor(n, sh, dt))
        self.x_d = dr("x", [S, D], F32, "ExternalInput").ap()
        self.wT_d = {w: dr(w, [128, NCH * D], F32, "ExternalInput").ap()
                     for w in ("wqT", "wkT", "wvT")}
        self.woT_d = dr("woT", [2, 128, NCH * 512], F32, "ExternalInput").ap()
        self.rw_d = dr("rw", [128, NCH * NE], F32, "ExternalInput").ap()
        self.wgu_d = dr("wgu", [NE, 128, 2 * NCH * MID_R], F32,
                        "ExternalInput").ap()
        self.wdr_d = dr("wdr", [NE, 128, 2 * D], F32, "ExternalInput").ap()
        self.wgs_d = dr("wgs", [NSH, 128, NCH * MID_S], F32, "ExternalInput").ap()
        self.wus_d = dr("wus", [NSH, 128, NCH * MID_S], F32, "ExternalInput").ap()
        self.wds_d = dr("wds", [NSH, 128, 4 * D], F32, "ExternalInput").ap()
        self.cvals_d = dr("cvals", [128, 4], F32, "ExternalInput").ap()
        self.ident_d = dr("ident", [128, 128], BF16, "ExternalInput").ap()
        self.pswap_d = dr("pswap", [128, 128], BF16, "ExternalInput").ap()
        self.onehot_d = dr("onehot", [16, NE * 128], BF16, "ExternalInput").ap()
        self.ctab_d = dr("ctab", [self.NTT, 128, NCH * 512], F32,
                         "ExternalInput").ap()
        self.stab_d = dr("stab", [self.NTT, 128, NCH * 512], F32,
                         "ExternalInput").ap()
        self.out_d = dr("out", [S, D], F32, "ExternalOutput").ap()
        self.xnsp_d = dr("xn_sp", [self.NTT, 128, NCH * 512], BF16).ap()
        self.wgu_bf = dr("wgu_bf", [NE, 128, 2 * NCH * MID_R], BF16).ap()
        self.wq_bf = dr("wq_bf", [128, NCH * D], BF16).ap()
        self.wv_bf = dr("wv_bf", [128, NCH * D], BF16).ap()
        self.rw_bf = dr("rw_bf", [128, NCH * NE], BF16).ap()
        self.wdr_bf = dr("wdr_bf", [NE, 128, 2 * D], BF16).ap()
        self.cbsp_d = dr("comb_sp", [16, S], BF16).ap()
        self.atsp_d = dr("attn_sp", [S, D], BF16).ap()

        self._g_rot = 0
        self._y_rot = 0
        self._a_rot = 0
        self.ident = self._sb(self.root, "identS", [128, 128], BF16)
        self.pswap = self._sb(self.root, "pswapS", [128, 128], BF16)
        self.cvals = self._sb(self.root, "cvalsS", [128, 4], F32)
        for j, v in enumerate((0.0, 1.0, EPS_LN, SCALE)):
            nc.const_aps.register(F32, v, self.cvals[:, j:j + 1])
        self.span = ExitStack()
        self.Kfm = self._sb(self.span, "Kfm", [128, NCH * S], BF16)
        self.Vs = self._sb(self.span, "Vs", [128, self.NKC * D], BF16)
        self.pb = [self.root.enter_context(
            nc.psum_tensor("pb%d" % i, [128, 512], F32)) for i in range(8)]
        self.pbf = [Buf() for _ in range(8)]

    def _sb(self, stack, name, shape, dt):
        t = stack.enter_context(self.nc.sbuf_tensor(name, shape, dt))
        self.bufs[name] = Buf()
        return t

    def new_phase(self):
        if self.sp is not None:
            self.sp.close()
        self.sp = ExitStack()
        toks = self.P.all_toks()
        for e in ENGS:
            self.P.op(e, "_wait_only", waits=toks)

    def sb(self, name, shape, dt):
        return self._sb(self.sp, name, shape, dt)

    def B(self, name):
        return self.bufs[name]

    def mark(self, eng, tok):
        return ("@", eng, self.P.pos[eng] - 1, tok)

    def gload(self, pairs, waits=()):
        s = "dG%d" % (self._g_rot % NGS)
        self._g_rot += 1
        tok = None
        first = True
        for dst, src in pairs:
            tok = self.P.op("G", "dma_start", dst, src,
                            waits=(list(waits) + [self.P.tok(s)])
                                  if first else (),
                            inc=s, inc_amt=16, dma=True)
            first = False
        return tok

    def yd(self, dst, src, waits=()):
        s = "dY%d" % (self._y_rot % NYS)
        self._y_rot += 1
        return self.P.op("Y", "dma_start", dst, src,
                         waits=list(waits) + [self.P.tok(s)],
                         inc=s, inc_amt=16, dma=True)

    def ad(self, dst, src, waits=()):
        s = "dA%d" % (self._a_rot % NAS)
        self._a_rot += 1
        return self.P.op("S", "dma_start", dst, src,
                         waits=list(waits) + [self.P.tok(s)],
                         inc=s, inc_amt=16, dma=True)

    def mm_group(self, bank, out_ap, lhsT_rhs_list, waits=()):
        bf = self.pbf[bank]
        w0 = bf.begin() + list(waits)
        n = len(lhsT_rhs_list)
        tok = None
        for i, (l, r) in enumerate(lhsT_rhs_list):
            tok = self.P.op("T", "matmul", out_ap, l, r,
                            start=(i == 0), stop=(i == n - 1),
                            waits=w0 if i == 0 else (),
                            inc="T" if i == n - 1 else None)
        bf.wrote(self.mark("T", tok))
        return tok

    def bank_read(self, bank, tok):
        self.pbf[bank].read(tok)

    def transpose(self, bank, src_ap, waits=(), out_parts=128):
        bf = self.pbf[bank]
        w0 = bf.begin() + list(waits) + self.B("identS").rd()
        tok = self.P.op("T", "matmul", self.pb[bank][:out_parts, 0:128],
                        src_ap, self.ident[:, :], start=True, stop=True,
                        waits=w0, inc="T")
        bf.wrote(self.mark("T", tok))
        return tok

    def phase_b1(self):
        P, S = self.P, self.S
        self.new_phase()
        wkT = self.sb("b_wkT", [128, NCH * D], BF16)
        x_sub = [self.sb("b_x%d" % i, [128, D], F32) for i in range(2)]
        sq = self.sb("b_sq", [128, D], BF16)
        stt = [self.sb("b_st%d" % i, [128, 1], F32) for i in range(6)]
        nbr_b = [self.sb("b_nbr%d" % i, [128, 2], F32) for i in range(2)]
        xn_tok_b = [self.sb("b_xntok%d" % i, [128, D], BF16)
                    for i in range(2)]
        xnt_b = [self.sb("b_xnt%d" % i, [128, NCH * 512], BF16)
                 for i in range(2)]
        kraw = [self.sb("b_kraw%d" % i, [128, 512], BF16) for i in range(2)]
        kswp = self.sb("b_kswp", [128, 512], BF16)
        kt1 = self.sb("b_kt1", [128, 512], BF16)
        ct = [self.sb("b_ct%d" % i, [128, 512], F32) for i in range(2)]
        st = [self.sb("b_stl%d" % i, [128, 512], F32) for i in range(2)]

        self._xnsp_toks = []
        tok_wk = self.gload([(wkT[:, :], self.wT_d["wkT"])])
        self.tok_cvq = self.gload([(self.wq_bf, self.wT_d["wqT"])])
        self.tok_cvv = self.gload([(self.wv_bf, self.wT_d["wvT"])])
        self.tok_cvr = self.gload([(self.rw_bf, self.rw_d)])
        tcv = self.yd(self.cvals[:, :], self.cvals_d)
        self.P.op("S", "_wait_only", waits=[tcv])
        t1 = self.yd(self.ident[:, :], self.ident_d)
        t2 = self.yd(self.pswap[:, :], self.pswap_d)
        self.B("identS").wrote(t1)
        self.B("pswapS").wrote(t2)

        def emit_ln(tt, s4):
                t0 = tt * 512 + s4 * 128
                xb, bxb = x_sub[s4 % 2], self.B("b_x%d" % (s4 % 2))
                ldx = self.yd(xb[:, :], self.x_d[t0:t0 + 128, :],
                              waits=bxb.begin())
                bxb.wrote(ldx)
                bsq, bs0, bs1 = self.B("b_sq"), self.B("b_st0"), self.B("b_st1")
                bs2, bs4 = self.B("b_st2"), self.B("b_st4")
                bnb = self.B("b_nbr%d" % (s4 % 2))
                nbr = nbr_b[s4 % 2]
                tsq = P.op("S", "activation", sq[:, :], xb[:, :], AF.Square,
                           accum_out=stt[0][:, :],
                           waits=[ldx] + bsq.begin() + bs0.begin(), inc="S")
                bsq.wrote(self.mark("S", tsq))
                bs0.wrote(self.mark("S", tsq))
                tsm = P.op("V", "reduce_sum", stt[1][:, :], xb[:, :], AX.X,
                           waits=[ldx] + bs1.begin(), inc="V")
                bs1.wrote(self.mark("V", tsm))
                bxb.read(self.mark("V", tsm))
                bxb.read(self.mark("S", tsq))
                tmn = P.op("V", "tensor_scalar_mul", stt[2][:, :], stt[1][:, :],
                           1.0 / D, waits=bs2.begin() + bs1.rd(), inc="V")
                bs2.wrote(self.mark("V", tmn))
                tm2 = P.op("V", "tensor_mul", stt[3][:, :], stt[2][:, :],
                           stt[2][:, :],
                           waits=self.B("b_st3").begin() + bs2.rd(), inc="V")
                self.B("b_st3").wrote(self.mark("V", tm2))
                tv1 = P.op("V", "tensor_scalar", stt[4][:, :], stt[0][:, :],
                           1.0 / D, stt[3][:, :], ALU.mult, ALU.subtract,
                           waits=bs4.begin() + bs0.rd() +
                                 self.B("b_st3").rd(), inc="V")
                self.B("b_st3").read(self.mark("V", tv1))
                tv2 = P.op("V", "tensor_scalar_max", stt[4][:, :], stt[4][:, :],
                           0.0, waits=[self.mark("V", tv1)], inc="V")
                bs4.wrote(self.mark("V", tv2))
                trs = P.op("S", "activation", nbr[:, 1:2], stt[4][:, :],
                           AF.Sqrt, bias=EPS_LN,
                           waits=bnb.begin() + bs4.rd(), inc="S")
                bs4.read(self.mark("S", trs))
                bnb.wrote(self.mark("S", trs))
                trc = P.op("V", "reciprocal", nbr[:, 1:2], nbr[:, 1:2],
                           waits=[self.mark("S", trs)], inc="V")
                bnb.wrote(self.mark("V", trc))
                tnb = P.op("V", "tensor_scalar", nbr[:, 0:1], stt[2][:, :],
                           nbr[:, 1:2], -1.0, ALU.mult, ALU.mult,
                           waits=[self.mark("V", trc)] + bs2.rd(), inc="V")
                bs2.read(self.mark("V", tnb))
                bnb.wrote(self.mark("V", tnb))
                self._xn_w[s4 % 2] = [self.mark("V", tnb)]

        def emit_xn(tt, s4):
                p = s4 % 2
                xb, bxb = x_sub[p], self.B("b_x%d" % p)
                xn_tok, bxn = xn_tok_b[p], self.B("b_xntok%d" % p)
                bnb, nbr = self.B("b_nbr%d" % p), nbr_b[p]
                txn = P.op("V", "tensor_scalar", xn_tok[:, :], xb[:, :],
                           nbr[:, 1:2], nbr[:, 0:1], ALU.mult, ALU.add,
                           waits=bxn.begin() + self._xn_w[p], inc="V")
                bxn.wrote(self.mark("V", txn))
                bxb.read(self.mark("V", txn))
                bnb.read(self.mark("V", txn))

        def emit_trans(tt, grp):
            s4 = grp // 2
            xnt = xnt_b[tt % 2]
            bxt = self.B("b_xnt%d" % (tt % 2))
            xn_tok = xn_tok_b[s4 % 2]
            bxn = self.B("b_xntok%d" % (s4 % 2))
            for c in range((grp % 2) * 4, (grp % 2) * 4 + 4):
                bk = c % 2
                tmm = self.transpose(bk, xn_tok[:, c * 128:(c + 1) * 128],
                                     waits=bxn.rd() + self._wxt[tt % 2])
                bxn.read(self.mark("T", tmm))
                col = c * 512 + s4 * 128
                tev = P.op("S", "copy", xnt[:, col:col + 128],
                           self.pb[bk][:, 0:128],
                           waits=self.pbf[bk].rd(), inc="S")
                self.bank_read(bk, self.mark("S", tev))
                bxt.wrote(self.mark("S", tev))

        self._wxt = [None, None]
        self._xn_w = [None, None]
        for tt in range(self.NTT + 1):
            cur = tt if tt < self.NTT else None
            prv = tt - 1 if tt >= 1 else None
            if cur is not None:
                self._wxt[cur % 2] = self.B("b_xnt%d" % (cur % 2)).begin()
            pend = None
            tgrp = 0
            for j in range(NCH):
                if cur is not None and j % 2 == 0:
                    emit_ln(cur, j // 2)
                if prv is not None:
                    pcs = j % 2
                    xnt_p = xnt_b[prv % 2]
                    bxt_p = self.B("b_xnt%d" % (prv % 2))
                    bct, bst = self.B("b_ct%d" % pcs), self.B("b_stl%d" % pcs)
                    tc1 = self.yd(ct[pcs][:, :],
                                  self.ctab_d[prv, :, j * 512:(j + 1) * 512],
                                  waits=bct.begin())
                    bct.wrote(tc1)
                    tc2 = self.yd(st[pcs][:, :],
                                  self.stab_d[prv, :, j * 512:(j + 1) * 512],
                                  waits=bst.begin())
                    bst.wrote(tc2)
                    bank = (2, 3, 6, 7)[j % 4]
                    tproj = self.mm_group(
                        bank, self.pb[bank][:, :],
                        [(wkT[:, c * D + j * 128:c * D + j * 128 + 128],
                          xnt_p[:, c * 512:(c + 1) * 512]) for c in range(NCH)],
                        waits=bxt_p.rd() + [tok_wk])
                    bxt_p.read(self.mark("T", tproj))
                    if pend is not None:
                        self.rope_step(*pend, tbuf=(kt1, self.B("b_kt1")))
                    pend = (bank, kraw[j % 2], kswp, ct[pcs], st[pcs],
                            "b_kraw%d" % (j % 2), "b_kswp", "b_ct%d" % pcs,
                            "b_stl%d" % pcs, 4 + (j % 2),
                            self.Kfm, j * S + prv * 512, "Kfm")
                if cur is not None and j >= 2:
                    if j % 2 == 0:
                        emit_xn(cur, j // 2 - 1)
                    elif j == 7:
                        emit_xn(cur, 3)
                    emit_trans(cur, tgrp)
                    tgrp += 1
            if prv is not None:
                self.rope_step(*pend, tbuf=(kt1, self.B("b_kt1")))
            if cur is not None:
                while tgrp < 8:
                    emit_trans(cur, tgrp)
                    tgrp += 1
                xnt_c = xnt_b[cur % 2]
                bxt_c = self.B("b_xnt%d" % (cur % 2))
                tsp = self.ad(self.xnsp_d[cur, :, :], xnt_c[:, :],
                              waits=bxt_c.rd())
                bxt_c.read(tsp)
                self._xnsp_toks.append(tsp)
        self.tok_xnsp = [t for t in self._xnsp_toks]

    def rope_step(self, bank, raw, swp, ctile, stile, nraw, nswp, nct, nst,
                  swap_bank, dst, dcol, ndst, dst_w=(), tbuf=None):
        P = self.P
        braw, bswp = self.B(nraw), self.B(nswp)
        bct, bst = self.B(nct), self.B(nst)
        bdst = self.B(ndst)
        te = P.op("S", "copy", raw[:, :], self.pb[bank][:, :],
                  waits=self.pbf[bank].rd() + braw.begin(), inc="S")
        braw.wrote(self.mark("S", te))
        self.bank_read(bank, self.mark("S", te))
        sbf = self.pbf[swap_bank]
        w0 = sbf.begin() + braw.rd() + self.B("pswapS").rd()
        tm = P.op("T", "matmul", self.pb[swap_bank][:, :], self.pswap[:, :],
                  raw[:, :], start=True, stop=True, waits=w0, inc="T")
        sbf.wrote(self.mark("T", tm))
        braw.read(self.mark("T", tm))
        tv = P.op("V", "tensor_copy", swp[:, :], self.pb[swap_bank][:, :],
                  waits=sbf.rd() + bswp.begin(), inc="V")
        bswp.wrote(self.mark("V", tv))
        self.bank_read(swap_bank, self.mark("V", tv))
        dsl = dst[:, dcol:dcol + 512]
        if tbuf is not None:
            tb, btb = tbuf
            t1 = P.op("V", "tensor_mul", tb[:, :], raw[:, :], ctile[:, :],
                      waits=braw.rd() + bct.rd() + btb.begin(), inc="V")
            btb.wrote(self.mark("V", t1))
            braw.read(self.mark("V", t1))
            bct.read(self.mark("V", t1))
            t2 = P.op("V", "tensor_mul", swp[:, :], swp[:, :], stile[:, :],
                      waits=bst.rd() + [self.mark("V", tv)], inc="V")
            bst.read(self.mark("V", t2))
            t3 = P.op("V", "tensor_add", dsl, tb[:, :], swp[:, :],
                      waits=[self.mark("V", t1), self.mark("V", t2)] +
                            list(dst_w), inc="V")
            btb.read(self.mark("V", t3))
        else:
            t1 = P.op("V", "tensor_mul", dsl, raw[:, :], ctile[:, :],
                      waits=braw.rd() + bct.rd() + list(dst_w), inc="V")
            braw.read(self.mark("V", t1))
            bct.read(self.mark("V", t1))
            t2 = P.op("V", "tensor_mul", swp[:, :], swp[:, :], stile[:, :],
                      waits=bst.rd() + [self.mark("V", tv)], inc="V")
            bst.read(self.mark("V", t2))
            t3 = P.op("V", "tensor_add", dsl, dsl, swp[:, :],
                      waits=[self.mark("V", t1), self.mark("V", t2)],
                      inc="V")
        bswp.read(self.mark("V", t3))
        tok = self.mark("V", t3)
        bdst.wrote(tok)
        return tok

    def phase_b2(self):
        P, S = self.P, self.S
        self.new_phase()
        wvT = self.sb("c_wvT", [128, NCH * D], BF16)
        xnt_b = [self.sb("c_xnt%d" % i, [128, NCH * 512], BF16)
                 for i in range(2)]

        tok_wv = [None, None]
        for h in range(2):
            ws = [self.yd(wvT[:, c * D + h * 512:c * D + h * 512 + 512],
                          self.wv_bf[:, c * D + h * 512:c * D + h * 512 + 512],
                          waits=[self.tok_cvv])
                  for c in range(NCH)]
            tok_wv[h] = ws
        for e in range(NE):
            self.gload([(self.wgu_bf[e], self.wgu_d[e])])
            self.gload([(self.wdr_bf[e], self.wdr_d[e])])
        self.tok_conv = [self.P.tok("dG%d" % i) for i in range(NGS)]

        for tt in range(self.NTT):
            xnt = xnt_b[tt % 2]
            bxt = self.B("c_xnt%d" % (tt % 2))
            if tt == 0:
                tldx = self.yd(xnt[:, :], self.xnsp_d[tt, :, :],
                               waits=self.tok_xnsp + bxt.begin())
                bxt.wrote(tldx)
            else:
                tldx = self._cxn_next
            if tt + 1 < self.NTT:
                bxt2 = self.B("c_xnt%d" % ((tt + 1) % 2))
                self._cxn_next = self.yd(xnt_b[(tt + 1) % 2][:, :],
                                         self.xnsp_d[tt + 1, :, :],
                                         waits=self.tok_xnsp + bxt2.begin())
                bxt2.wrote(self._cxn_next)
            for s4 in range(4):
                kc = tt * 4 + s4
                q0 = s4 * 128
                for h in range(2):
                    bank = (kc % 2) * 2 + h
                    tpv = self.mm_group(
                        bank, self.pb[bank][:, :],
                        [(xnt[:, c * 512 + q0:c * 512 + q0 + 128],
                          wvT[:, c * D + h * 512:c * D + h * 512 + 512])
                         for c in range(NCH)],
                        waits=bxt.rd() + tok_wv[h])
                    bxt.read(self.mark("T", tpv))
                    tev = P.op("S", "copy",
                               self.Vs[:, kc * D + h * 512:kc * D + h * 512 + 512],
                               self.pb[bank][:, :], waits=self.pbf[bank].rd(),
                               inc="S")
                    self.bank_read(bank, self.mark("S", tev))
                    self.B("Vs").wrote(self.mark("S", tev))


    def phase_d(self):
        P, S = self.P, self.S
        NKB = self.NKB
        self.new_phase()
        wqT = self.sb("d_wqT", [128, NCH * D], BF16)
        woTh = [self.sb("d_woTh%d" % h, [128, NCH * 512], BF16)
                for h in range(2)]
        xnt = self.sb("d_xnt", [128, NCH * 512], BF16)
        qfm = self.sb("d_qfm", [128, NCH * 512], BF16)
        ct = [self.sb("d_ct%d" % i, [128, 512], F32) for i in range(2)]
        st = [self.sb("d_stl%d" % i, [128, 512], F32) for i in range(2)]
        Pb = [self.sb("d_P0", [128, S], BF16)]
        PT = self.sb("d_PT", [128, 4 * 128], BF16)
        OT = self.sb("d_OT", [128, NCH * 512], BF16)
        rws = self.sb("d_rws", [128, NCH * NE], BF16)
        lg4 = self.sb("d_lg4", [128, 4 * NE], F32)
        pe = self.sb("d_pe", [128, NE], F32)
        msk = self.sb("d_msk", [128, NE], F32)
        eq1 = self.sb("d_eq1", [128, NE], F32)
        comb = [self.sb("d_comb%d" % i, [128, NE], BF16) for i in range(2)]
        r6 = [self.sb("d_r%d" % i, [128, 1], F32) for i in range(5)]
        cbt = self.sb("d_cbt", [16, 512], BF16)
        Ot = self.sb("d_O", [128, D], BF16)
        qraw = self.sb("d_qraw", [128, 512], BF16)
        qswp = self.sb("d_qswp", [128, 512], BF16)
        att = [qraw, qswp]
        att_n = ["d_qraw", "d_qswp"]
        nmh = [self.sb("d_nmh%d" % i, [128, 1], F32) for i in range(2)]
        lpart = self.sb("d_lp", [128, 8], F32)
        lall = self.sb("d_la", [128, 1], F32)
        nmp = self.sb("d_nmp", [128, 1], F32)

        self._at_toks = []
        tok_rws = self.yd(rws[:, :], self.rw_bf, waits=[self.tok_cvr])
        tok_wqj = []
        for j in range(NCH):
            ws = [self.yd(wqT[:, c * D + j * 128:c * D + j * 128 + 128],
                          self.wq_bf[:, c * D + j * 128:c * D + j * 128 + 128],
                          waits=[self.tok_cvq])
                  for c in range(NCH)]
            tok_wqj.append(ws)
        tok_wo = [self.gload([(woTh[h][:, :], self.woT_d[h])])
                  for h in range(2)]
        t0m = P.op("V", "memset", nmp[:, :], 0.0, inc="V")
        self.B("d_nmp").wrote(self.mark("V", t0m))
        self._rt_v_last = None
        self._rt_s_prev = [None] * 4
        self._cb_pend = []
        self.tok_cbsp = []

        self._cbt_spill = None
        self._cbt_copies = []

        def emit_combT(kc, pcm):
            bcm = self.B("d_comb%d" % pcm)
            tct = self.transpose(0, comb[pcm][:, :], waits=bcm.rd(),
                                 out_parts=16)
            bcm.read(self.mark("T", tct))
            tcv = P.op("S", "copy",
                       cbt[:16, (kc % 4) * 128:(kc % 4) * 128 + 128],
                       self.pb[0][:16, 0:128],
                       waits=self.pbf[0].rd() + [self._cbt_spill],
                       inc="S")
            self.bank_read(0, self.mark("S", tcv))
            self._cbt_copies.append(self.mark("S", tcv))

        def spill_cbt(tile):
            tspc = self.yd(self.cbsp_d[:16, tile * 512:tile * 512 + 512],
                           cbt[:16, :], waits=self._cbt_copies)
            self._cbt_copies = []
            self._cbt_spill = tspc
            self.tok_cbsp.append(tspc)

        for qt in range(self.NTT):
            bxt = self.B("d_xnt")
            if qt == 0:
                tldx = self.yd(xnt[:, :], self.xnsp_d[qt, :, :],
                               waits=self.tok_xnsp + bxt.begin())
                bxt.wrote(tldx)
            else:
                tldx = self._dxn_next
            for s4 in range(4):
                q0 = s4 * 128
                rb = s4
                trt = self.mm_group(
                    rb, self.pb[rb][:, 0:NE],
                    [(xnt[:, c * 512 + q0:c * 512 + q0 + 128],
                      rws[:, c * NE:(c + 1) * NE]) for c in range(NCH)],
                    waits=bxt.rd() + [tok_rws])
                bxt.read(self.mark("T", trt))
                tlg = P.op("V", "tensor_copy",
                           lg4[:, s4 * NE:(s4 + 1) * NE],
                           self.pb[rb][:, 0:NE],
                           waits=self.pbf[rb].rd() + [self._rt_s_prev[s4]],
                           inc="V")
                self.bank_read(rb, self.mark("V", tlg))
            bqf = self.B("d_qfm")
            wq0 = bqf.begin()
            qfm_w = [None] * NCH
            pend = None
            for j in range(NCH):
                pcs = j % 2
                bct, bst = self.B("d_ct%d" % pcs), self.B("d_stl%d" % pcs)
                tc1 = self.yd(ct[pcs][:, :],
                              self.ctab_d[qt, :, j * 512:(j + 1) * 512],
                              waits=bct.begin())
                bct.wrote(tc1)
                tc2 = self.yd(st[pcs][:, :],
                              self.stab_d[qt, :, j * 512:(j + 1) * 512],
                              waits=bst.begin())
                bst.wrote(tc2)
                bank = (6, 7, 2, 3)[j % 4]
                tproj = self.mm_group(
                    bank, self.pb[bank][:, :],
                    [(wqT[:, c * D + j * 128:c * D + j * 128 + 128],
                      xnt[:, c * 512:(c + 1) * 512]) for c in range(NCH)],
                    waits=bxt.rd() + tok_wqj[j])
                bxt.read(self.mark("T", tproj))
                if pend is not None:
                    qfm_w[pend[0]] = self.rope_step(*pend[1:], dst_w=wq0)
                pend = (j, bank, qraw, qswp, ct[pcs], st[pcs],
                        "d_qraw", "d_qswp", "d_ct%d" % pcs,
                        "d_stl%d" % pcs, 4 + (j % 2),
                        qfm, j * 512, "d_qfm")
            qfm_w[pend[0]] = self.rope_step(*pend[1:], dst_w=wq0)
            if self._cb_pend:
                emit_combT(*self._cb_pend.pop(0))
                spill_cbt(qt - 1)
            if qt + 1 < self.NTT:
                self._dxn_next = self.yd(xnt[:, :], self.xnsp_d[qt + 1, :, :],
                                         waits=self.tok_xnsp + bxt.begin())
                bxt.wrote(self._dxn_next)
            for s4 in range(4):
                pp = 0
                Pbuf, bP = Pb[0], self.B("d_P0")
                wP = bP.begin()
                q0 = s4 * 128
                exp_tok = [None, None]
                wlp = self.B("d_lp").begin()
                for hf in range(2):
                    banks = [hf * 4 + kb for kb in range(NKB)]
                    bws = [self.pbf[banks[kb]].begin() for kb in range(NKB)]
                    tks = []
                    for j in range(NCH):
                        for kb in range(NKB):
                            kcol = (hf * NKB + kb) * 512
                            w = ()
                            if j == 0:
                                w = list(bws[kb])
                                if kb == 0:
                                    w += [qfm_w[0]] + self.B("Kfm").rd()
                            elif kb == 0:
                                w = [qfm_w[j]]
                            tk = P.op(
                                "T", "matmul", self.pb[banks[kb]][:, :],
                                qfm[:, j * 512 + q0:j * 512 + q0 + 128],
                                self.Kfm[:, j * S + kcol:j * S + kcol + 512],
                                start=(j == 0), stop=(j == NCH - 1),
                                waits=w,
                                inc="T" if j == NCH - 1 else None)
                            if j == NCH - 1:
                                self.pbf[banks[kb]].wrote(self.mark("T", tk))
                                tks.append(tk)
                    blp = self.B("d_lp")
                    bnp = self.B("d_nmp")
                    for kb in range(NKB):
                        kcol = (hf * NKB + kb) * 512
                        li = hf * NKB + kb
                        tex = P.op("S", "activation", Pbuf[:, kcol:kcol + 512],
                                   self.pb[banks[kb]][:, :], AF.Exp,
                                   bias=nmp[:, :], scale=SCALE,
                                   accum_out=lpart[:, li:li + 1],
                                   waits=[tks[kb]] + bnp.rd() + wP + wlp +
                                         self.pbf[banks[kb]].rd(),
                                   inc="S")
                        self.bank_read(banks[kb], self.mark("S", tex))
                        bP.wrote(self.mark("S", tex))
                        blp.wrote(self.mark("S", tex))
                        bnp.read(self.mark("S", tex))
                    exp_tok[hf] = P.tok("S")
                def vop(meth, *a, waits=(), **kw):
                    return P.op("V", meth, *a,
                                waits=list(waits) + [("@", "V", P.pos["V"] - 1,
                                                      ("V", P.count["V"]))],
                                inc="V", **kw)
                t_l = vop("reduce_sum", lall[:, :], lpart[:, 0:2 * NKB], AX.X,
                          waits=[exp_tok[0], exp_tok[1]])
                self.B("d_lp").read(self.mark("V", t_l))
                trec = vop("reciprocal", lall[:, :], lall[:, :])
                tln = P.op("S", "activation", nmh[1][:, :], lall[:, :],
                           AF.Ln, waits=[self.mark("V", trec)] +
                           self.B("d_nmh1").begin(), inc="S")
                self.B("d_la").read(self.mark("S", tln))
                self.B("d_nmh1").wrote(self.mark("S", tln))
                tpr = vop("tensor_tensor", nmp[:, :], nmp[:, :],
                          nmh[1][:, :], ALU.add,
                          waits=[self.mark("S", tln)] +
                                self.B("d_nmp").begin())
                self.B("d_nmh1").read(self.mark("V", tpr))
                tpr = vop("tensor_scalar", nmp[:, :], nmp[:, :], 12.0, -12.0,
                          ALU.min, ALU.max)
                self.B("d_nmp").wrote(self.mark("V", tpr))
                LAG = 4
                tpv = [None, None]
                ogrp = [self.pbf[2].begin(), self.pbf[3].begin()]
                ptv_tok = {}
                for step in range(self.NKC + LAG):
                    kc = step - LAG
                    if kc >= 0:
                        slot = kc % 4
                        for h in range(2):
                            tpv[h] = P.op(
                                "T", "matmul", self.pb[2 + h][:, :],
                                PT[:, slot * 128:slot * 128 + 128],
                                self.Vs[:, kc * D + h * 512:kc * D + h * 512 + 512],
                                start=(kc == 0), stop=(kc == self.NKC - 1),
                                waits=([ptv_tok[kc]] + ogrp[h] +
                                       self.B("Vs").rd()) if kc == 0 else
                                      [ptv_tok[kc]],
                                inc="T" if kc == self.NKC - 1 else None)
                    kc_t = step
                    if kc_t < self.NKC:
                        bk = (0, 1, 4, 5)[kc_t % 4]
                        tmm = self.transpose(
                            bk, Pbuf[:, kc_t * 128:kc_t * 128 + 128],
                            waits=[exp_tok[0] if kc_t < self.NKC // 2
                                   else exp_tok[1]])
                        bP.read(self.mark("T", tmm))
                        slot = kc_t % 4
                        tptv = P.op("V", "tensor_copy",
                                    PT[:, slot * 128:slot * 128 + 128],
                                    self.pb[bk][:, 0:128],
                                    waits=self.pbf[bk].rd(),
                                    inc="V")
                        self.bank_read(bk, self.mark("V", tptv))
                        ptv_tok[kc_t] = self.mark("V", tptv)
                self.pbf[2].wrote(self.mark("T", tpv[0]))
                self.pbf[3].wrote(self.mark("T", tpv[1]))
                bO = self.B("d_O")
                wO = bO.begin()
                tevs = []
                for h in range(2):
                    tev = P.op("V", "tensor_scalar_mul",
                               Ot[:, h * 512:h * 512 + 512],
                               self.pb[2 + h][:, :], lall[:, :],
                               waits=self.pbf[2 + h].rd() + wO +
                                     [self.mark("V", trec)], inc="V")
                    self.bank_read(2 + h, self.mark("V", tev))
                    bO.wrote(self.mark("V", tev))
                    tevs.append(tev)
                bOT = self.B("d_OT")
                if s4 == 0:
                    self._wOT = bOT.begin()
                for c in range(NCH):
                    bk = c % 2
                    tmm = self.transpose(bk, Ot[:, c * 128:c * 128 + 128],
                                         waits=bO.rd())
                    bO.read(self.mark("T", tmm))
                    col = c * 512 + s4 * 128
                    tv = P.op("V", "tensor_copy", OT[:, col:col + 128],
                              self.pb[bk][:, 0:128],
                              waits=self.pbf[bk].rd() + self._wOT, inc="V")
                    self.bank_read(bk, self.mark("V", tv))
                    bOT.wrote(self.mark("V", tv))
                for h in range(2):
                    pat = h
                    bat = self.B(att_n[pat])
                    wat = bat.begin()
                    bank = 4 + h
                    twv = self.mm_group(
                        bank, self.pb[bank][:, :],
                        [(OT[:, c * 512 + s4 * 128:c * 512 + s4 * 128 + 128],
                          woTh[h][:, c * 512:(c + 1) * 512])
                         for c in range(NCH)],
                        waits=bOT.rd() + [tok_wo[h]])
                    bOT.read(self.mark("T", twv))
                    tve = P.op("V", "tensor_copy", att[pat][:, :],
                               self.pb[bank][:, :],
                               waits=self.pbf[bank].rd() + wat, inc="V")
                    self.bank_read(bank, self.mark("V", tve))
                    bat.wrote(self.mark("V", tve))
                    t0 = qt * 512 + s4 * 128
                    tsp = self.yd(
                        self.atsp_d[t0:t0 + 128, h * 512:h * 512 + 512],
                        att[pat][:, :], waits=bat.rd())
                    bat.read(tsp)
                    self._at_toks.append(tsp)
                kc = qt * 4 + s4
                pcm = kc % 2
                if len(self._cb_pend) >= 1:
                    emit_combT(*self._cb_pend.pop(0))
                lgs = lg4[:, s4 * NE:(s4 + 1) * NE]
                vop("reduce_max", r6[0][:, :], lgs, AX.X, negate=True)
                tnm2 = P.tok("V")
                tex2 = P.op("S", "activation", pe[:, :], lgs, AF.Exp,
                            bias=r6[0][:, :], accum_out=r6[1][:, :],
                            waits=[tnm2, self._rt_v_last], inc="S")
                self._rt_s_prev[s4] = self.mark("S", tex2)
                vop("reciprocal", r6[1][:, :], r6[1][:, :], waits=[tex2])
                vop("tensor_scalar_mul", pe[:, :], pe[:, :], r6[1][:, :])
                vop("reduce_max", r6[2][:, :], pe[:, :], AX.X)
                vop("tensor_scalar", msk[:, :], pe[:, :], r6[2][:, :], None,
                    ALU.is_lt)
                vop("tensor_mul", msk[:, :], msk[:, :], pe[:, :])
                vop("reduce_max", r6[3][:, :], msk[:, :], AX.X)
                vop("tensor_scalar", r6[4][:, :], r6[2][:, :], r6[3][:, :],
                    EPS_RT, ALU.add, ALU.add)
                vop("reciprocal", r6[4][:, :], r6[4][:, :])
                vop("tensor_scalar", r6[0][:, :], r6[2][:, :], r6[4][:, :],
                    None, ALU.mult)
                vop("tensor_scalar", r6[1][:, :], r6[3][:, :], r6[4][:, :],
                    None, ALU.mult)
                vop("tensor_scalar", eq1[:, :], pe[:, :], r6[2][:, :],
                    r6[0][:, :], ALU.is_ge, ALU.mult)
                vop("tensor_scalar", msk[:, :], msk[:, :], r6[3][:, :],
                    r6[1][:, :], ALU.is_ge, ALU.mult)
                bcm = self.B("d_comb%d" % pcm)
                tcb = vop("tensor_add", comb[pcm][:, :], eq1[:, :],
                          msk[:, :], waits=bcm.begin())
                self._rt_v_last = tcb
                bcm.wrote(self.mark("V", tcb))
                self._cb_pend.append((kc, pcm))
        while self._cb_pend:
            emit_combT(*self._cb_pend.pop(0))
        spill_cbt(self.NTT - 1)
        self.tok_atsp = list(self._at_toks)

    def phase_f(self):
        P, S = self.P, self.S
        self.new_phase()
        self.span.close()
        xnt_b = [self.sb("f_xnt%d" % i, [128, NCH * 512], BF16)
                 for i in range(2)]
        WGU = 2 * NCH * MID_R
        wgu = [self.sb("f_wgu%d" % i, [128, WGU], BF16) for i in range(2)]
        wg = [w[:, 0:NCH * MID_R] for w in wgu]
        wu = [w[:, NCH * MID_R:2 * NCH * MID_R] for w in wgu]
        wd = [self.sb("f_wd%d" % i, [128, 2 * D], BF16) for i in range(2)]
        wgs = self.sb("f_wgs", [128, NSH * NCH * MID_S], BF16)
        wus = self.sb("f_wus", [128, NSH * NCH * MID_S], BF16)
        wds = self.sb("f_wds", [128, NSH * 4 * D], BF16)
        cb = [self.sb("f_cb%d" % i, [128, 512], F32) for i in range(2)]
        sg = self.sb("f_sg", [128, 2 * 512], BF16)
        uu = self.sb("f_uu", [128, 2 * 512], BF16)
        sgs = self.sb("f_sgs", [128, 4 * 512], BF16)
        uus = self.sb("f_uus", [128, 4 * 512], BF16)
        hb = [self.sb("f_h%d" % e, [128, 2 * 512], BF16) for e in range(NE)]
        hs = [self.sb("f_hs%d" % n, [128, 4 * 512], BF16) for n in range(NSH)]
        onehot = self.sb("f_onehot", [16, NE * 128], BF16)
        combT = self.sb("f_combT", [16, S], BF16)
        x_sub = [self.sb("f_x%d" % i, [128, D], F32) for i in range(4)]
        at_sub = [self.sb("f_at%d" % i, [128, D], BF16) for i in range(4)]
        acc = [self.sb("f_acc%d" % i, [128, D], F32) for i in range(2)]

        def load_gu(e, g=False):
            bw = self.B("f_wgu%d" % (e % 2))
            w = self.tok_conv + bw.begin()
            if g:
                t = self.gload([(wgu[e % 2][:, :], self.wgu_bf[e])], waits=w)
            else:
                t = self.yd(wgu[e % 2][:, :], self.wgu_bf[e], waits=w)
            bw.wrote(t)
            return t

        tok_gu = {}
        tok_gu[0] = load_gu(0, g=True)
        tok_gu[1] = load_gu(1, g=True)
        t_oh = self.yd(onehot[:16, :], self.onehot_d)
        self.bufs["f_onehot"].wrote(t_oh)
        t_cbt = [self.yd(combT[:16, tt * 512:tt * 512 + 512],
                         self.cbsp_d[:16, tt * 512:tt * 512 + 512],
                         waits=[self.tok_cbsp[tt]])
                 for tt in range(self.NTT)]
        prs = []
        for n in range(NSH):
            prs.append((wgs[:, n * NCH * MID_S:(n + 1) * NCH * MID_S],
                        self.wgs_d[n]))
            prs.append((wus[:, n * NCH * MID_S:(n + 1) * NCH * MID_S],
                        self.wus_d[n]))
            prs.append((wds[:, n * 4 * D:(n + 1) * 4 * D], self.wds_d[n]))
        tok_ws = self.gload(prs)

        def load_wd(e):
            bwd = self.B("f_wd%d" % (e % 2))
            t = self.yd(wd[e % 2][:, :], self.wdr_bf[e],
                        waits=self.tok_conv + bwd.begin())
            bwd.wrote(t)
            return t

        tok_wd = {}

        def emit_cb(e, tt):
            bank = e % 2
            bf = self.pbf[bank]
            bcb = self.B("f_cb%d" % (e % 2))
            w0 = (bf.begin() + [t_cbt[tt]] +
                  self.B("f_onehot").rd())
            tcb = P.op("T", "matmul", self.pb[bank][:, :],
                       onehot[:16, e * 128:e * 128 + 128],
                       combT[:16, tt * 512:tt * 512 + 512],
                       start=True, stop=True, waits=w0, inc="T")
            bf.wrote(self.mark("T", tcb))
            tcbe = P.op("V", "tensor_copy", cb[e % 2][:, :],
                        self.pb[bank][:, :],
                        waits=bf.rd() + bcb.begin(), inc="V")
            self.bank_read(bank, self.mark("V", tcbe))
            bcb.wrote(self.mark("V", tcbe))

        for tt in range(self.NTT):
            if tt != 0:
                tok_gu[0], tok_gu[1] = self._gu_next
            xnt = xnt_b[tt % 2]
            bxt = self.B("f_xnt%d" % (tt % 2))
            if tt == 0:
                tldx = self.yd(xnt[:, :], self.xnsp_d[tt, :, :],
                               waits=self.tok_xnsp + bxt.begin())
                bxt.wrote(tldx)
            else:
                tldx = self._xn_next
            if tt + 1 < self.NTT:
                bxt2 = self.B("f_xnt%d" % ((tt + 1) % 2))
                self._xn_next = self.yd(xnt_b[(tt + 1) % 2][:, :],
                                        self.xnsp_d[tt + 1, :, :],
                                        waits=self.tok_xnsp + bxt2.begin())
                bxt2.wrote(self._xn_next)
            if tt == 0:
                emit_cb(0, tt)
                emit_cb(1, tt)
            for e in range(NE):
                pe_ = e % 2
                bcb = self.B("f_cb%d" % pe_)
                bhe = self.B("f_h%d" % e)
                bwg = self.B("f_wgu%d" % pe_)
                bwu = bwg
                whe = bhe.begin()
                for mc in range(2):
                    bg, bu = 2 + (2 * e) % 6, 2 + (2 * e + 1) % 6
                    tg = self.mm_group(
                        bg, self.pb[bg][:, :],
                        [(wg[pe_][:, c * MID_R + mc * 128:c * MID_R + mc * 128 + 128],
                          xnt[:, c * 512:(c + 1) * 512]) for c in range(NCH)],
                        waits=bxt.rd() + [tok_gu[e]])
                    bxt.read(self.mark("T", tg))
                    bwg.read(self.mark("T", tg))
                    tu = self.mm_group(
                        bu, self.pb[bu][:, :],
                        [(wu[pe_][:, c * MID_R + mc * 128:c * MID_R + mc * 128 + 128],
                          xnt[:, c * 512:(c + 1) * 512]) for c in range(NCH)],
                        waits=bxt.rd() + [tok_gu[e]])
                    bxt.read(self.mark("T", tu))
                    bwu.read(self.mark("T", tu))
                    bsg, buu = self.B("f_sg"), self.B("f_uu")
                    tsg = P.op("S", "activation", sg[:, mc * 512:mc * 512 + 512],
                               self.pb[bg][:, :], AF.Silu,
                               waits=self.pbf[bg].rd() +
                                     (bsg.begin() if mc == 0 else []), inc="S")
                    self.bank_read(bg, self.mark("S", tsg))
                    bsg.wrote(self.mark("S", tsg))
                    tuc = P.op("V", "scalar_tensor_tensor",
                               uu[:, mc * 512:mc * 512 + 512],
                               self.pb[bu][:, :], 1.0, cb[pe_][:, :],
                               ALU.mult, ALU.mult,
                               waits=self.pbf[bu].rd() + bcb.rd() +
                                     (buu.begin() if mc == 0 else []), inc="V")
                    self.bank_read(bu, self.mark("V", tuc))
                    bcb.read(self.mark("V", tuc))
                    buu.wrote(self.mark("V", tuc))
                    th = P.op("V", "tensor_mul",
                              hb[e][:, mc * 512:mc * 512 + 512],
                              sg[:, mc * 512:mc * 512 + 512],
                              uu[:, mc * 512:mc * 512 + 512],
                              waits=bsg.rd() + buu.rd() + whe +
                                    [self.mark("V", tuc)], inc="V")
                    bsg.read(self.mark("V", th))
                    buu.read(self.mark("V", th))
                    bhe.wrote(self.mark("V", th))
                if e + 2 < NE:
                    emit_cb(e + 2, tt)
                    tok_gu[e + 2] = load_gu(e + 2)
            for n in range(NSH):
                bhn = self.B("f_hs%d" % n)
                whn = bhn.begin()
                for mc in range(4):
                    bg = 2 + (2 * (NE + n)) % 6
                    bu = 2 + (2 * (NE + n) + 1) % 6
                    base = lambda c: (n * NCH + c) * MID_S + mc * 128
                    tg = self.mm_group(
                        bg, self.pb[bg][:, :],
                        [(wgs[:, base(c):base(c) + 128],
                          xnt[:, c * 512:(c + 1) * 512]) for c in range(NCH)],
                        waits=bxt.rd() + [tok_ws])
                    bxt.read(self.mark("T", tg))
                    tu = self.mm_group(
                        bu, self.pb[bu][:, :],
                        [(wus[:, base(c):base(c) + 128],
                          xnt[:, c * 512:(c + 1) * 512]) for c in range(NCH)],
                        waits=bxt.rd() + [tok_ws])
                    bxt.read(self.mark("T", tu))
                    bsg, buu = self.B("f_sgs"), self.B("f_uus")
                    tsg = P.op("S", "activation",
                               sgs[:, mc * 512:mc * 512 + 512],
                               self.pb[bg][:, :], AF.Silu,
                               waits=self.pbf[bg].rd() +
                                     (bsg.begin() if mc == 0 and n == 0 else []),
                               inc="S")
                    self.bank_read(bg, self.mark("S", tsg))
                    bsg.wrote(self.mark("S", tsg))
                    tuc = P.op("V", "tensor_copy",
                               uus[:, mc * 512:mc * 512 + 512],
                               self.pb[bu][:, :],
                               waits=self.pbf[bu].rd() +
                                     (buu.begin() if mc == 0 and n == 0 else []),
                               inc="V")
                    self.bank_read(bu, self.mark("V", tuc))
                    buu.wrote(self.mark("V", tuc))
                    th = P.op("V", "tensor_mul",
                              hs[n][:, mc * 512:mc * 512 + 512],
                              sgs[:, mc * 512:mc * 512 + 512],
                              uus[:, mc * 512:mc * 512 + 512],
                              waits=bsg.rd() + buu.rd() + whn +
                                    [self.mark("V", tuc)], inc="V")
                    bsg.read(self.mark("V", th))
                    buu.read(self.mark("V", th))
                    bhn.wrote(self.mark("V", th))
            if tt + 1 < self.NTT:
                emit_cb(0, tt + 1)
                emit_cb(1, tt + 1)
            ep_tok = {}
            for s4 in range(4):
                t0 = tt * 512 + s4 * 128
                bx, ba = self.B("f_x%d" % s4), self.B("f_at%d" % s4)
                tx = self.yd(x_sub[s4][:, :], self.x_d[t0:t0 + 128, :],
                             waits=bx.begin())
                bx.wrote(tx)
                ta = self.yd(at_sub[s4][:, :], self.atsp_d[t0:t0 + 128, :],
                             waits=self.tok_atsp + ba.begin())
                ba.wrote(ta)
                ep_tok[s4] = (tx, ta)
            if tt + 1 < self.NTT:
                self._gu_next = (load_gu(0), load_gu(1))
            h_rd = []
            for e in range(NE):
                h_rd += self.B("f_h%d" % e).rd()
            for n in range(NSH):
                h_rd += self.B("f_hs%d" % n).rd()
            grp_w = [self.pbf[b].begin() for b in range(8)]

            def down_routed():
                tok_wd[0] = load_wd(0)
                tok_wd[1] = load_wd(1)
                for e in range(NE):
                    for s4 in range(4):
                        for h in range(2):
                            b = s4 * 2 + h
                            for mc in range(2):
                                first = (e == 0 and mc == 0)
                                last_of_e = (s4 == 3 and h == 1 and mc == 1)
                                tok = P.op(
                                    "T", "matmul", self.pb[b][:, :],
                                    hb[e][:, mc * 512 + s4 * 128:mc * 512 + s4 * 128 + 128],
                                    wd[e % 2][:, mc * D + h * 512:mc * D + h * 512 + 512],
                                    start=first, stop=False,
                                    waits=(grp_w[b] + h_rd + [tok_wd[e]])
                                          if first else
                                          ([tok_wd[e]] if mc == 0 and
                                           s4 == 0 and h == 0 else ()),
                                    inc="T" if last_of_e else None)
                    self.B("f_wd%d" % (e % 2)).read(self.mark("T", tok))
                    if e + 2 < NE:
                        tok_wd[e + 2] = load_wd(e + 2)

            def down_shared(s4):
                for n in range(NSH):
                    for h in range(2):
                        b = s4 * 2 + h
                        for mc in range(4):
                            last = (n == NSH - 1 and mc == 3)
                            tok = P.op(
                                "T", "matmul", self.pb[b][:, :],
                                hs[n][:, mc * 512 + s4 * 128:mc * 512 + s4 * 128 + 128],
                                wds[:, (n * 4 + mc) * D + h * 512:(n * 4 + mc) * D + h * 512 + 512],
                                start=False, stop=last,
                                waits=[tok_ws] if n == 0 and mc == 0 and
                                      h == 0 else (),
                                inc="T" if last else None)
                            if last:
                                self.pbf[b].wrote(self.mark("T", tok))

            def epilogue(s4):
                t0 = tt * 512 + s4 * 128
                px = s4 % 2
                bx, ba, bac = (self.B("f_x%d" % s4), self.B("f_at%d" % s4),
                               self.B("f_acc%d" % px))
                tx, ta = ep_tok[s4]
                wac = bac.begin()
                for h in range(2):
                    b = s4 * 2 + h
                    tv1 = P.op("V", "tensor_add",
                               acc[px][:, h * 512:h * 512 + 512],
                               self.pb[b][:, :],
                               x_sub[s4][:, h * 512:h * 512 + 512],
                               waits=self.pbf[b].rd() + [tx] + wac, inc="V")
                    self.bank_read(b, self.mark("V", tv1))
                    bx.read(self.mark("V", tv1))
                    tv2 = P.op("V", "tensor_add",
                               acc[px][:, h * 512:h * 512 + 512],
                               acc[px][:, h * 512:h * 512 + 512],
                               at_sub[s4][:, h * 512:h * 512 + 512],
                               waits=[ta, self.mark("V", tv1)], inc="V")
                    ba.read(self.mark("V", tv2))
                    bac.wrote(self.mark("V", tv2))
                tst = self.yd(self.out_d[t0:t0 + 128, :], acc[px][:, :],
                              waits=bac.rd())
                bac.read(tst)

            down_routed()
            for s4 in (1, 2, 0, 3):
                down_shared(s4)
                epilogue(s4)
            tok_T = P.tok("T")
            for e in range(NE):
                self.B("f_h%d" % e).read(tok_T)
            for n in range(NSH):
                self.B("f_hs%d" % n).read(tok_T)
            bxt.read(tok_T)
        P.op("Y", "_wait_only", waits=[P.tok("dY%d" % i) for i in range(NYS)])

    def build(self):
        self.phase_b1()
        self.phase_b2()
        self.phase_d()
        self.phase_f()
        if self.sp is not None:
            self.sp.close()
        nc, P = self.nc, self.P
        sems = {}
        for s in SEMS:
            sems[s] = self.root.enter_context(nc.semaphore(s))
        with nc.Block() as block:
            @block.tensor
            def _(eng):
                P.replay("T", eng, sems)

            @block.vector
            def _(eng):
                P.replay("V", eng, sems)

            @block.scalar
            def _(eng):
                P.replay("S", eng, sems)

            @block.sync
            def _(eng):
                P.replay("Y", eng, sems)

            @block.gpsimd
            def _(eng):
                P.replay("G", eng, sems)
        self.root.close()
        return nc


def chunk_major(w):
    C = w.shape[0] // 128
    return np.ascontiguousarray(
        w.reshape(C, 128, w.shape[1]).transpose(1, 0, 2).reshape(128, -1))


def prep_weights(wq, wk, wv, wo, rw, wgr, wur, wdr, wgs, wus, wds):
    woT = np.asarray(wo, np.float32).T
    return {
        "wqT": chunk_major(np.ascontiguousarray(np.asarray(wq, np.float32).T)),
        "wkT": chunk_major(np.ascontiguousarray(np.asarray(wk, np.float32).T)),
        "wvT": chunk_major(np.ascontiguousarray(np.asarray(wv, np.float32).T)),
        "woT": np.stack([chunk_major(np.ascontiguousarray(woT[:, h * 512:(h + 1) * 512]))
                         for h in range(2)]),
        "rw": chunk_major(np.asarray(rw, np.float32)),
        "wgu": np.stack([np.concatenate(
            [chunk_major(np.asarray(wgr[e], np.float32)),
             chunk_major(np.asarray(wur[e], np.float32))], axis=1)
            for e in range(NE)]),
        "wdr": np.stack([chunk_major(np.asarray(wdr[e], np.float32))
                         for e in range(NE)]),
        "wgs": np.stack([chunk_major(np.asarray(wgs[n], np.float32))
                         for n in range(NSH)]),
        "wus": np.stack([chunk_major(np.asarray(wus[n], np.float32))
                         for n in range(NSH)]),
        "wds": np.stack([chunk_major(np.asarray(wds[n], np.float32))
                         for n in range(NSH)]),
    }


def host_constants(S):
    NTT = S // 512
    ident = np.eye(128, dtype=np.float32)
    pswap = np.zeros((128, 128), dtype=np.float32)
    for i in range(64):
        pswap[2 * i, 2 * i + 1] = 1.0
        pswap[2 * i + 1, 2 * i] = 1.0
    onehot = np.zeros((16, NE * 128), dtype=np.float32)
    for e in range(NE):
        onehot[e, e * 128:(e + 1) * 128] = 1.0
    inv = 1.0 / (10000.0 ** (np.arange(0, D, 2, dtype=np.float32) / D))
    t = np.arange(S, dtype=np.float32)
    ang = np.outer(t, inv)
    cos = np.cos(ang).astype(np.float32)
    sin = np.sin(ang).astype(np.float32)
    Cfm = np.empty((D, S), dtype=np.float32)
    Sfm = np.empty((D, S), dtype=np.float32)
    Cfm[0::2, :] = cos.T
    Cfm[1::2, :] = cos.T
    Sfm[0::2, :] = -sin.T
    Sfm[1::2, :] = sin.T

    def tiles(X):
        out = np.empty((NTT, 128, NCH * 512), dtype=np.float32)
        for tt in range(NTT):
            for c in range(NCH):
                out[tt, :, c * 512:(c + 1) * 512] = \
                    X[c * 128:(c + 1) * 128, tt * 512:(tt + 1) * 512]
        return out

    cvals = np.zeros((128, 4), dtype=np.float32)
    cvals[:, 0] = 0.0
    cvals[:, 1] = 1.0
    cvals[:, 2] = EPS_LN
    cvals[:, 3] = SCALE
    return {
        "cvals": cvals,
        "ident": ident.astype(BFNP),
        "pswap": pswap.astype(BFNP),
        "onehot": onehot.astype(BFNP),
        "ctab": tiles(Cfm),
        "stab": tiles(Sfm),
    }


_NC_CACHE = {}


def get_nc(S=4096):
    if S not in _NC_CACHE:
        _NC_CACHE[S] = Builder(S).build()
    return _NC_CACHE[S]


def kernel(L0_attn_x, L0_attn_wq, L0_attn_wk, L0_attn_wv, L0_attn_wo,
           L0_ffn_router_w, L0_ffn_wg_r, L0_ffn_wu_r, L0_ffn_wd_r,
           L0_ffn_wg_s, L0_ffn_wu_s, L0_ffn_wd_s):
    S = 4096
    x = np.asarray(L0_attn_x, dtype=np.float32)
    base = prep_weights(L0_attn_wq, L0_attn_wk, L0_attn_wv, L0_attn_wo,
                        L0_ffn_router_w, L0_ffn_wg_r, L0_ffn_wu_r,
                        L0_ffn_wd_r, L0_ffn_wg_s, L0_ffn_wu_s, L0_ffn_wd_s)
    consts = host_constants(S)
    nc = get_nc(S)
    in_maps = []
    for i in range(8):
        m = {"x": np.ascontiguousarray(x[i])}
        m.update(base)
        m.update(consts)
        in_maps.append(m)
    res = run_bass_kernel_spmd(nc, in_maps, core_ids=list(range(8)))
    return np.stack([np.asarray(res.results[i]["out"]) for i in range(8)],
                    axis=0)


# version 47
# speedup vs baseline: 1.1259x; 1.1259x over previous
import math
from contextlib import ExitStack

import numpy as np
import ml_dtypes

import concourse.bass as bass
import concourse.mybir as mybir
from concourse.bass_utils import run_bass_kernel_spmd

F32 = mybir.dt.float32
BF16 = mybir.dt.bfloat16
AF = mybir.ActivationFunctionType
ALU = mybir.AluOpType
AX = mybir.AxisListType
BFNP = ml_dtypes.bfloat16

D = 1024
NCH = 8
NE = 16
MID_R = 256
MID_S = 512
NSH = 2
SCALE = 1.0 / 32.0
EPS_LN = 1e-5
EPS_RT = 1e-9

ENGS = ("T", "V", "S", "Y", "G")
NYS, NAS, NGS = 8, 4, 4
SEMS = (("T", "V", "S")
        + tuple("dY%d" % i for i in range(NYS))
        + tuple("dA%d" % i for i in range(NAS))
        + tuple("dG%d" % i for i in range(NGS)))


class Plan:

    def __init__(self):
        self.ops = {e: [] for e in ENGS}
        self.count = {s: 0 for s in SEMS}
        self.waited = {}
        self.pos = {e: 0 for e in ENGS}
        self.drained = {e: 0 for e in ENGS}

    def op(self, eng, meth, *args, waits=(), inc=None, inc_amt=1,
           dma=False, **kw):
        w = []
        drain = False
        for tok in waits:
            if tok is None:
                continue
            if tok[0] == "@":
                _, teng, tpos, stok = tok
                if teng == eng:
                    if eng == "T" or (eng == "S" and not dma):
                        continue
                    if stok is not None:
                        tok = stok
                    else:
                        if tpos >= self.drained[eng]:
                            drain = True
                            self.drained[eng] = self.pos[eng]
                        continue
                tok = stok
                if tok is None:
                    raise RuntimeError(
                        "cross-engine dep %s->%s without a semaphore token"
                        % (teng, eng))
            s, v = tok
            if v <= 0:
                continue
            key = (eng, s)
            if self.waited.get(key, 0) >= v:
                continue
            self.waited[key] = v
            w.append((s, v))
        incs = []
        tok = None
        if inc is not None:
            self.count[inc] += inc_amt
            tok = (inc, self.count[inc])
            incs.append((inc, inc_amt))
        self.ops[eng].append((meth, args, kw, tuple(w), tuple(incs), drain))
        self.pos[eng] += 1
        return tok

    def tok(self, s):
        return (s, self.count[s])

    def all_toks(self):
        return [(s, self.count[s]) for s in SEMS]

    def replay(self, eng_name, eng, sems):
        for meth, args, kw, waits, incs, drain in self.ops[eng_name]:
            if drain:
                eng.drain()
            for s, v in waits:
                eng.wait_ge(sems[s], v)
            if meth == "_wait_only":
                continue
            inst = getattr(eng, meth)(*args, **kw)
            for s, amt in incs:
                inst = inst.then_inc(sems[s], amt)


class Buf:

    def __init__(self):
        self.w = []
        self.r = []
        self.gen_waits = []

    def begin(self):
        self.gen_waits = self.w + self.r
        self.w = []
        self.r = []
        return list(self.gen_waits)

    def wrote(self, tok):
        if tok is not None:
            self.w.append(tok)

    def rd(self):
        return list(self.w)

    def read(self, tok):
        if tok is not None:
            self.r.append(tok)


class Builder:
    def __init__(self, S):
        assert S % 1024 == 0
        self.S = S
        self.NTT = S // 512
        self.NKC = S // 128
        self.NKB = S // 1024
        nc = bass.Bass("TRN2", target_bir_lowering=False, debug=False)
        self.nc = nc
        self.P = Plan()
        self.root = ExitStack()
        self.sp = None
        self.bufs = {}

        dr = lambda n, sh, dt, k=None: (
            nc.dram_tensor(n, sh, dt, kind=k) if k else nc.dram_tensor(n, sh, dt))
        self.x_d = dr("x", [S, D], F32, "ExternalInput").ap()
        self.wT_d = {w: dr(w, [128, NCH * D], F32, "ExternalInput").ap()
                     for w in ("wqT", "wkT", "wvT")}
        self.woT_d = dr("woT", [2, 128, NCH * 512], F32, "ExternalInput").ap()
        self.rw_d = dr("rw", [128, NCH * NE], F32, "ExternalInput").ap()
        self.wgu_d = dr("wgu", [NE, 128, 2 * NCH * MID_R], F32,
                        "ExternalInput").ap()
        self.wdr_d = dr("wdr", [NE, 128, 2 * D], F32, "ExternalInput").ap()
        self.wgs_d = dr("wgs", [NSH, 128, NCH * MID_S], F32, "ExternalInput").ap()
        self.wus_d = dr("wus", [NSH, 128, NCH * MID_S], F32, "ExternalInput").ap()
        self.wds_d = dr("wds", [NSH, 128, 4 * D], F32, "ExternalInput").ap()
        self.cvals_d = dr("cvals", [128, 4], F32, "ExternalInput").ap()
        self.ident_d = dr("ident", [128, 128], BF16, "ExternalInput").ap()
        self.pswap_d = dr("pswap", [128, 128], BF16, "ExternalInput").ap()
        self.onehot_d = dr("onehot", [16, NE * 128], BF16, "ExternalInput").ap()
        self.ctab_d = dr("ctab", [self.NTT, 128, NCH * 512], F32,
                         "ExternalInput").ap()
        self.stab_d = dr("stab", [self.NTT, 128, NCH * 512], F32,
                         "ExternalInput").ap()
        self.out_d = dr("out", [S, D], F32, "ExternalOutput").ap()
        self.xnsp_d = dr("xn_sp", [self.NTT, 128, NCH * 512], BF16).ap()
        self.wgu_bf = dr("wgu_bf", [NE, 128, 2 * NCH * MID_R], BF16).ap()
        self.wq_bf = dr("wq_bf", [128, NCH * D], BF16).ap()
        self.wv_bf = dr("wv_bf", [128, NCH * D], BF16).ap()
        self.rw_bf = dr("rw_bf", [128, NCH * NE], BF16).ap()
        self.wdr_bf = dr("wdr_bf", [NE, 128, 2 * D], BF16).ap()
        self.cbsp_d = dr("comb_sp", [16, S], BF16).ap()
        self.atsp_d = dr("attn_sp", [S, D], BF16).ap()

        self._g_rot = 0
        self._y_rot = 0
        self._a_rot = 0
        self.ident = self._sb(self.root, "identS", [128, 128], BF16)
        self.pswap = self._sb(self.root, "pswapS", [128, 128], BF16)
        self.cvals = self._sb(self.root, "cvalsS", [128, 4], F32)
        for j, v in enumerate((0.0, 1.0, EPS_LN, SCALE)):
            nc.const_aps.register(F32, v, self.cvals[:, j:j + 1])
        self.span = ExitStack()
        self.Kfm = self._sb(self.span, "Kfm", [128, NCH * S], BF16)
        self.Vs = self._sb(self.span, "Vs", [128, self.NKC * D], BF16)
        self.pb = [self.root.enter_context(
            nc.psum_tensor("pb%d" % i, [128, 512], F32)) for i in range(8)]
        self.pbf = [Buf() for _ in range(8)]

    def _sb(self, stack, name, shape, dt):
        t = stack.enter_context(self.nc.sbuf_tensor(name, shape, dt))
        self.bufs[name] = Buf()
        return t

    def new_phase(self):
        if self.sp is not None:
            self.sp.close()
        self.sp = ExitStack()
        toks = self.P.all_toks()
        for e in ENGS:
            self.P.op(e, "_wait_only", waits=toks)

    def sb(self, name, shape, dt):
        return self._sb(self.sp, name, shape, dt)

    def B(self, name):
        return self.bufs[name]

    def mark(self, eng, tok):
        return ("@", eng, self.P.pos[eng] - 1, tok)

    def gload(self, pairs, waits=()):
        s = "dG%d" % (self._g_rot % NGS)
        self._g_rot += 1
        tok = None
        first = True
        for dst, src in pairs:
            tok = self.P.op("G", "dma_start", dst, src,
                            waits=(list(waits) + [self.P.tok(s)])
                                  if first else (),
                            inc=s, inc_amt=16, dma=True)
            first = False
        return tok

    def yd(self, dst, src, waits=()):
        s = "dY%d" % (self._y_rot % NYS)
        self._y_rot += 1
        return self.P.op("Y", "dma_start", dst, src,
                         waits=list(waits) + [self.P.tok(s)],
                         inc=s, inc_amt=16, dma=True)

    def ad(self, dst, src, waits=()):
        s = "dA%d" % (self._a_rot % NAS)
        self._a_rot += 1
        return self.P.op("S", "dma_start", dst, src,
                         waits=list(waits) + [self.P.tok(s)],
                         inc=s, inc_amt=16, dma=True)

    def mm_group(self, bank, out_ap, lhsT_rhs_list, waits=()):
        bf = self.pbf[bank]
        w0 = bf.begin() + list(waits)
        n = len(lhsT_rhs_list)
        tok = None
        for i, (l, r) in enumerate(lhsT_rhs_list):
            tok = self.P.op("T", "matmul", out_ap, l, r,
                            start=(i == 0), stop=(i == n - 1),
                            waits=w0 if i == 0 else (),
                            inc="T" if i == n - 1 else None)
        bf.wrote(self.mark("T", tok))
        return tok

    def bank_read(self, bank, tok):
        self.pbf[bank].read(tok)

    def transpose(self, bank, src_ap, waits=(), out_parts=128):
        bf = self.pbf[bank]
        w0 = bf.begin() + list(waits) + self.B("identS").rd()
        tok = self.P.op("T", "matmul", self.pb[bank][:out_parts, 0:128],
                        src_ap, self.ident[:, :], start=True, stop=True,
                        waits=w0, inc="T")
        bf.wrote(self.mark("T", tok))
        return tok

    def phase_b1(self):
        P, S = self.P, self.S
        self.new_phase()
        wkT = self.sb("b_wkT", [128, NCH * D], BF16)
        x_sub = [self.sb("b_x%d" % i, [128, D], F32) for i in range(2)]
        sq = self.sb("b_sq", [128, D], BF16)
        stt = [self.sb("b_st%d" % i, [128, 1], F32) for i in range(6)]
        nbr_b = [self.sb("b_nbr%d" % i, [128, 2], F32) for i in range(2)]
        xn_tok_b = [self.sb("b_xntok%d" % i, [128, D], BF16)
                    for i in range(2)]
        xnt_b = [self.sb("b_xnt%d" % i, [128, NCH * 512], BF16)
                 for i in range(2)]
        kraw = [self.sb("b_kraw%d" % i, [128, 512], BF16) for i in range(2)]
        kswp = self.sb("b_kswp", [128, 512], BF16)
        kt1 = self.sb("b_kt1", [128, 512], BF16)
        ct = [self.sb("b_ct%d" % i, [128, 512], F32) for i in range(2)]
        st = [self.sb("b_stl%d" % i, [128, 512], F32) for i in range(2)]

        self._xnsp_toks = []
        tok_wk = self.gload([(wkT[:, :], self.wT_d["wkT"])])
        self.tok_cvq = self.gload([(self.wq_bf, self.wT_d["wqT"])])
        self.tok_cvv = self.gload([(self.wv_bf, self.wT_d["wvT"])])
        self.tok_cvr = self.gload([(self.rw_bf, self.rw_d)])
        tcv = self.yd(self.cvals[:, :], self.cvals_d)
        self.P.op("S", "_wait_only", waits=[tcv])
        t1 = self.yd(self.ident[:, :], self.ident_d)
        t2 = self.yd(self.pswap[:, :], self.pswap_d)
        self.B("identS").wrote(t1)
        self.B("pswapS").wrote(t2)

        def emit_ln(tt, s4):
                t0 = tt * 512 + s4 * 128
                xb, bxb = x_sub[s4 % 2], self.B("b_x%d" % (s4 % 2))
                ldx = self.yd(xb[:, :], self.x_d[t0:t0 + 128, :],
                              waits=bxb.begin())
                bxb.wrote(ldx)
                bsq, bs0, bs1 = self.B("b_sq"), self.B("b_st0"), self.B("b_st1")
                bs2, bs4 = self.B("b_st2"), self.B("b_st4")
                bnb = self.B("b_nbr%d" % (s4 % 2))
                nbr = nbr_b[s4 % 2]
                tsq = P.op("S", "activation", sq[:, :], xb[:, :], AF.Square,
                           accum_out=stt[0][:, :],
                           waits=[ldx] + bsq.begin() + bs0.begin(), inc="S")
                bsq.wrote(self.mark("S", tsq))
                bs0.wrote(self.mark("S", tsq))
                tsm = P.op("V", "reduce_sum", stt[1][:, :], xb[:, :], AX.X,
                           waits=[ldx] + bs1.begin(), inc="V")
                bs1.wrote(self.mark("V", tsm))
                bxb.read(self.mark("V", tsm))
                bxb.read(self.mark("S", tsq))
                tmn = P.op("V", "tensor_scalar_mul", stt[2][:, :], stt[1][:, :],
                           1.0 / D, waits=bs2.begin() + bs1.rd(), inc="V")
                bs2.wrote(self.mark("V", tmn))
                tm2 = P.op("V", "tensor_mul", stt[3][:, :], stt[2][:, :],
                           stt[2][:, :],
                           waits=self.B("b_st3").begin() + bs2.rd(), inc="V")
                self.B("b_st3").wrote(self.mark("V", tm2))
                tv1 = P.op("V", "tensor_scalar", stt[4][:, :], stt[0][:, :],
                           1.0 / D, stt[3][:, :], ALU.mult, ALU.subtract,
                           waits=bs4.begin() + bs0.rd() +
                                 self.B("b_st3").rd(), inc="V")
                self.B("b_st3").read(self.mark("V", tv1))
                tv2 = P.op("V", "tensor_scalar_max", stt[4][:, :], stt[4][:, :],
                           0.0, waits=[self.mark("V", tv1)], inc="V")
                bs4.wrote(self.mark("V", tv2))
                trs = P.op("S", "activation", nbr[:, 1:2], stt[4][:, :],
                           AF.Sqrt, bias=EPS_LN,
                           waits=bnb.begin() + bs4.rd(), inc="S")
                bs4.read(self.mark("S", trs))
                bnb.wrote(self.mark("S", trs))
                trc = P.op("V", "reciprocal", nbr[:, 1:2], nbr[:, 1:2],
                           waits=[self.mark("S", trs)], inc="V")
                bnb.wrote(self.mark("V", trc))
                tnb = P.op("V", "tensor_scalar", nbr[:, 0:1], stt[2][:, :],
                           nbr[:, 1:2], -1.0, ALU.mult, ALU.mult,
                           waits=[self.mark("V", trc)] + bs2.rd(), inc="V")
                bs2.read(self.mark("V", tnb))
                bnb.wrote(self.mark("V", tnb))
                self._xn_w[s4 % 2] = [self.mark("V", tnb)]

        def emit_xn(tt, s4):
                p = s4 % 2
                xb, bxb = x_sub[p], self.B("b_x%d" % p)
                xn_tok, bxn = xn_tok_b[p], self.B("b_xntok%d" % p)
                bnb, nbr = self.B("b_nbr%d" % p), nbr_b[p]
                txn = P.op("V", "tensor_scalar", xn_tok[:, :], xb[:, :],
                           nbr[:, 1:2], nbr[:, 0:1], ALU.mult, ALU.add,
                           waits=bxn.begin() + self._xn_w[p], inc="V")
                bxn.wrote(self.mark("V", txn))
                bxb.read(self.mark("V", txn))
                bnb.read(self.mark("V", txn))

        def emit_trans(tt, grp):
            s4 = grp // 2
            xnt = xnt_b[tt % 2]
            bxt = self.B("b_xnt%d" % (tt % 2))
            xn_tok = xn_tok_b[s4 % 2]
            bxn = self.B("b_xntok%d" % (s4 % 2))
            for c in range((grp % 2) * 4, (grp % 2) * 4 + 4):
                bk = c % 2
                tmm = self.transpose(bk, xn_tok[:, c * 128:(c + 1) * 128],
                                     waits=bxn.rd() + self._wxt[tt % 2])
                bxn.read(self.mark("T", tmm))
                col = c * 512 + s4 * 128
                tev = P.op("S", "copy", xnt[:, col:col + 128],
                           self.pb[bk][:, 0:128],
                           waits=self.pbf[bk].rd(), inc="S")
                self.bank_read(bk, self.mark("S", tev))
                bxt.wrote(self.mark("S", tev))

        self._wxt = [None, None]
        self._xn_w = [None, None]
        for tt in range(self.NTT + 1):
            cur = tt if tt < self.NTT else None
            prv = tt - 1 if tt >= 1 else None
            if cur is not None:
                self._wxt[cur % 2] = self.B("b_xnt%d" % (cur % 2)).begin()
            pend = None
            tgrp = 0
            for j in range(NCH):
                if cur is not None and j % 2 == 0:
                    emit_ln(cur, j // 2)
                if prv is not None:
                    pcs = j % 2
                    xnt_p = xnt_b[prv % 2]
                    bxt_p = self.B("b_xnt%d" % (prv % 2))
                    bct, bst = self.B("b_ct%d" % pcs), self.B("b_stl%d" % pcs)
                    tc1 = self.yd(ct[pcs][:, :],
                                  self.ctab_d[prv, :, j * 512:(j + 1) * 512],
                                  waits=bct.begin())
                    bct.wrote(tc1)
                    tc2 = self.yd(st[pcs][:, :],
                                  self.stab_d[prv, :, j * 512:(j + 1) * 512],
                                  waits=bst.begin())
                    bst.wrote(tc2)
                    bank = (2, 3, 6, 7)[j % 4]
                    tproj = self.mm_group(
                        bank, self.pb[bank][:, :],
                        [(wkT[:, c * D + j * 128:c * D + j * 128 + 128],
                          xnt_p[:, c * 512:(c + 1) * 512]) for c in range(NCH)],
                        waits=bxt_p.rd() + [tok_wk])
                    bxt_p.read(self.mark("T", tproj))
                    if pend is not None:
                        self.rope_step(*pend, tbuf=(kt1, self.B("b_kt1")))
                    pend = (bank, kraw[j % 2], kswp, ct[pcs], st[pcs],
                            "b_kraw%d" % (j % 2), "b_kswp", "b_ct%d" % pcs,
                            "b_stl%d" % pcs, 4 + (j % 2),
                            self.Kfm, j * S + prv * 512, "Kfm")
                if cur is not None and j >= 2:
                    if j % 2 == 0:
                        emit_xn(cur, j // 2 - 1)
                    elif j == 7:
                        emit_xn(cur, 3)
                    emit_trans(cur, tgrp)
                    tgrp += 1
            if prv is not None:
                self.rope_step(*pend, tbuf=(kt1, self.B("b_kt1")))
            if cur is not None:
                while tgrp < 8:
                    emit_trans(cur, tgrp)
                    tgrp += 1
                xnt_c = xnt_b[cur % 2]
                bxt_c = self.B("b_xnt%d" % (cur % 2))
                tsp = self.ad(self.xnsp_d[cur, :, :], xnt_c[:, :],
                              waits=bxt_c.rd())
                bxt_c.read(tsp)
                self._xnsp_toks.append(tsp)
        self.tok_xnsp = [t for t in self._xnsp_toks]

    def rope_step(self, bank, raw, swp, ctile, stile, nraw, nswp, nct, nst,
                  swap_bank, dst, dcol, ndst, dst_w=(), tbuf=None):
        P = self.P
        braw, bswp = self.B(nraw), self.B(nswp)
        bct, bst = self.B(nct), self.B(nst)
        bdst = self.B(ndst)
        te = P.op("S", "copy", raw[:, :], self.pb[bank][:, :],
                  waits=self.pbf[bank].rd() + braw.begin(), inc="S")
        braw.wrote(self.mark("S", te))
        self.bank_read(bank, self.mark("S", te))
        sbf = self.pbf[swap_bank]
        w0 = sbf.begin() + braw.rd() + self.B("pswapS").rd()
        tm = P.op("T", "matmul", self.pb[swap_bank][:, :], self.pswap[:, :],
                  raw[:, :], start=True, stop=True, waits=w0, inc="T")
        sbf.wrote(self.mark("T", tm))
        braw.read(self.mark("T", tm))
        tv = P.op("V", "tensor_copy", swp[:, :], self.pb[swap_bank][:, :],
                  waits=sbf.rd() + bswp.begin(), inc="V")
        bswp.wrote(self.mark("V", tv))
        self.bank_read(swap_bank, self.mark("V", tv))
        dsl = dst[:, dcol:dcol + 512]
        if tbuf is not None:
            tb, btb = tbuf
            t1 = P.op("V", "tensor_mul", tb[:, :], raw[:, :], ctile[:, :],
                      waits=braw.rd() + bct.rd() + btb.begin(), inc="V")
            btb.wrote(self.mark("V", t1))
            braw.read(self.mark("V", t1))
            bct.read(self.mark("V", t1))
            t2 = P.op("V", "tensor_mul", swp[:, :], swp[:, :], stile[:, :],
                      waits=bst.rd() + [self.mark("V", tv)], inc="V")
            bst.read(self.mark("V", t2))
            t3 = P.op("V", "tensor_add", dsl, tb[:, :], swp[:, :],
                      waits=[self.mark("V", t1), self.mark("V", t2)] +
                            list(dst_w), inc="V")
            btb.read(self.mark("V", t3))
        else:
            t1 = P.op("V", "tensor_mul", dsl, raw[:, :], ctile[:, :],
                      waits=braw.rd() + bct.rd() + list(dst_w), inc="V")
            braw.read(self.mark("V", t1))
            bct.read(self.mark("V", t1))
            t2 = P.op("V", "tensor_mul", swp[:, :], swp[:, :], stile[:, :],
                      waits=bst.rd() + [self.mark("V", tv)], inc="V")
            bst.read(self.mark("V", t2))
            t3 = P.op("V", "tensor_add", dsl, dsl, swp[:, :],
                      waits=[self.mark("V", t1), self.mark("V", t2)],
                      inc="V")
        bswp.read(self.mark("V", t3))
        tok = self.mark("V", t3)
        bdst.wrote(tok)
        return tok

    def phase_b2(self):
        P, S = self.P, self.S
        self.new_phase()
        wvT = self.sb("c_wvT", [128, NCH * D], BF16)
        xnt_b = [self.sb("c_xnt%d" % i, [128, NCH * 512], BF16)
                 for i in range(2)]

        HW = NCH * 512
        tok_wv = [[self.yd(wvT[:, h * HW:(h + 1) * HW],
                           self.wv_bf[:, h * HW:(h + 1) * HW],
                           waits=[self.tok_cvv])]
                  for h in range(2)]
        for e in range(NE):
            self.gload([(self.wgu_bf[e], self.wgu_d[e])])
            self.gload([(self.wdr_bf[e], self.wdr_d[e])])
        self.tok_conv = [self.P.tok("dG%d" % i) for i in range(NGS)]

        for tt in range(self.NTT):
            xnt = xnt_b[tt % 2]
            bxt = self.B("c_xnt%d" % (tt % 2))
            if tt == 0:
                tldx = self.yd(xnt[:, :], self.xnsp_d[tt, :, :],
                               waits=self.tok_xnsp + bxt.begin())
                bxt.wrote(tldx)
            else:
                tldx = self._cxn_next
            if tt + 1 < self.NTT:
                bxt2 = self.B("c_xnt%d" % ((tt + 1) % 2))
                self._cxn_next = self.yd(xnt_b[(tt + 1) % 2][:, :],
                                         self.xnsp_d[tt + 1, :, :],
                                         waits=self.tok_xnsp + bxt2.begin())
                bxt2.wrote(self._cxn_next)
            for s4 in range(4):
                kc = tt * 4 + s4
                q0 = s4 * 128
                for h in range(2):
                    bank = (kc % 2) * 2 + h
                    tpv = self.mm_group(
                        bank, self.pb[bank][:, :],
                        [(xnt[:, c * 512 + q0:c * 512 + q0 + 128],
                          wvT[:, h * HW + c * 512:h * HW + c * 512 + 512])
                         for c in range(NCH)],
                        waits=bxt.rd() + tok_wv[h])
                    bxt.read(self.mark("T", tpv))
                    tev = P.op("S", "copy",
                               self.Vs[:, kc * D + h * 512:kc * D + h * 512 + 512],
                               self.pb[bank][:, :], waits=self.pbf[bank].rd(),
                               inc="S")
                    self.bank_read(bank, self.mark("S", tev))
                    self.B("Vs").wrote(self.mark("S", tev))


    def phase_d(self):
        P, S = self.P, self.S
        NKB = self.NKB
        self.new_phase()
        wqT = self.sb("d_wqT", [128, NCH * D], BF16)
        woTh = [self.sb("d_woTh%d" % h, [128, NCH * 512], BF16)
                for h in range(2)]
        xnt = self.sb("d_xnt", [128, NCH * 512], BF16)
        qfm = self.sb("d_qfm", [128, NCH * 512], BF16)
        ct = [self.sb("d_ct%d" % i, [128, 512], F32) for i in range(2)]
        st = [self.sb("d_stl%d" % i, [128, 512], F32) for i in range(2)]
        Pb = [self.sb("d_P0", [128, S], BF16)]
        PT = self.sb("d_PT", [128, 4 * 128], BF16)
        OT = self.sb("d_OT", [128, NCH * 512], BF16)
        rws = self.sb("d_rws", [128, NCH * NE], BF16)
        lg4 = self.sb("d_lg4", [128, 4 * NE], F32)
        pe = self.sb("d_pe", [128, NE], F32)
        msk = self.sb("d_msk", [128, NE], F32)
        eq1 = self.sb("d_eq1", [128, NE], F32)
        comb = [self.sb("d_comb%d" % i, [128, NE], BF16) for i in range(2)]
        r6 = [self.sb("d_r%d" % i, [128, 1], F32) for i in range(5)]
        cbt = self.sb("d_cbt", [16, 512], BF16)
        Ot = self.sb("d_O", [128, D], BF16)
        qraw = self.sb("d_qraw", [128, 512], BF16)
        qswp = self.sb("d_qswp", [128, 512], BF16)
        att = [qraw, qswp]
        att_n = ["d_qraw", "d_qswp"]
        nmh = [self.sb("d_nmh%d" % i, [128, 1], F32) for i in range(2)]
        lpart = self.sb("d_lp", [128, 8], F32)
        lall = self.sb("d_la", [128, 1], F32)
        nmp = self.sb("d_nmp", [128, 1], F32)

        self._at_toks = []
        tok_rws = self.yd(rws[:, :], self.rw_bf, waits=[self.tok_cvr])
        tok_wqj = [[self.yd(wqT[:, j * 1024:(j + 1) * 1024],
                            self.wq_bf[:, j * 1024:(j + 1) * 1024],
                            waits=[self.tok_cvq])]
                   for j in range(NCH)]
        tok_wo = [self.gload([(woTh[h][:, :], self.woT_d[h])])
                  for h in range(2)]
        t0m = P.op("V", "memset", nmp[:, :], 0.0, inc="V")
        self.B("d_nmp").wrote(self.mark("V", t0m))
        self._rt_v_last = None
        self._rt_s_prev = [None] * 4
        self._cb_pend = []
        self.tok_cbsp = []

        self._cbt_spill = None
        self._cbt_copies = []

        def emit_combT(kc, pcm):
            bcm = self.B("d_comb%d" % pcm)
            tct = self.transpose(0, comb[pcm][:, :], waits=bcm.rd(),
                                 out_parts=16)
            bcm.read(self.mark("T", tct))
            tcv = P.op("S", "copy",
                       cbt[:16, (kc % 4) * 128:(kc % 4) * 128 + 128],
                       self.pb[0][:16, 0:128],
                       waits=self.pbf[0].rd() + [self._cbt_spill],
                       inc="S")
            self.bank_read(0, self.mark("S", tcv))
            self._cbt_copies.append(self.mark("S", tcv))

        def spill_cbt(tile):
            tspc = self.yd(self.cbsp_d[:16, tile * 512:tile * 512 + 512],
                           cbt[:16, :], waits=self._cbt_copies)
            self._cbt_copies = []
            self._cbt_spill = tspc
            self.tok_cbsp.append(tspc)

        for qt in range(self.NTT):
            bxt = self.B("d_xnt")
            if qt == 0:
                tldx = self.yd(xnt[:, :], self.xnsp_d[qt, :, :],
                               waits=self.tok_xnsp + bxt.begin())
                bxt.wrote(tldx)
            else:
                tldx = self._dxn_next
            for s4 in range(4):
                q0 = s4 * 128
                rb = s4
                trt = self.mm_group(
                    rb, self.pb[rb][:, 0:NE],
                    [(xnt[:, c * 512 + q0:c * 512 + q0 + 128],
                      rws[:, c * NE:(c + 1) * NE]) for c in range(NCH)],
                    waits=bxt.rd() + [tok_rws])
                bxt.read(self.mark("T", trt))
                tlg = P.op("V", "tensor_copy",
                           lg4[:, s4 * NE:(s4 + 1) * NE],
                           self.pb[rb][:, 0:NE],
                           waits=self.pbf[rb].rd() + [self._rt_s_prev[s4]],
                           inc="V")
                self.bank_read(rb, self.mark("V", tlg))
            bqf = self.B("d_qfm")
            wq0 = bqf.begin()
            qfm_w = [None] * NCH
            pend = None
            for j in range(NCH):
                pcs = j % 2
                bct, bst = self.B("d_ct%d" % pcs), self.B("d_stl%d" % pcs)
                tc1 = self.yd(ct[pcs][:, :],
                              self.ctab_d[qt, :, j * 512:(j + 1) * 512],
                              waits=bct.begin())
                bct.wrote(tc1)
                tc2 = self.yd(st[pcs][:, :],
                              self.stab_d[qt, :, j * 512:(j + 1) * 512],
                              waits=bst.begin())
                bst.wrote(tc2)
                bank = (6, 7, 2, 3)[j % 4]
                tproj = self.mm_group(
                    bank, self.pb[bank][:, :],
                    [(wqT[:, j * 1024 + c * 128:j * 1024 + c * 128 + 128],
                      xnt[:, c * 512:(c + 1) * 512]) for c in range(NCH)],
                    waits=bxt.rd() + tok_wqj[j])
                bxt.read(self.mark("T", tproj))
                if pend is not None:
                    qfm_w[pend[0]] = self.rope_step(*pend[1:], dst_w=wq0)
                pend = (j, bank, qraw, qswp, ct[pcs], st[pcs],
                        "d_qraw", "d_qswp", "d_ct%d" % pcs,
                        "d_stl%d" % pcs, 4 + (j % 2),
                        qfm, j * 512, "d_qfm")
            qfm_w[pend[0]] = self.rope_step(*pend[1:], dst_w=wq0)
            if self._cb_pend:
                emit_combT(*self._cb_pend.pop(0))
                spill_cbt(qt - 1)
            if qt + 1 < self.NTT:
                self._dxn_next = self.yd(xnt[:, :], self.xnsp_d[qt + 1, :, :],
                                         waits=self.tok_xnsp + bxt.begin())
                bxt.wrote(self._dxn_next)
            for s4 in range(4):
                pp = 0
                Pbuf, bP = Pb[0], self.B("d_P0")
                wP = bP.begin()
                q0 = s4 * 128
                exp_tok = [None, None]
                wlp = self.B("d_lp").begin()
                for hf in range(2):
                    banks = [hf * 4 + kb for kb in range(NKB)]
                    bws = [self.pbf[banks[kb]].begin() for kb in range(NKB)]
                    tks = []
                    for j in range(NCH):
                        for kb in range(NKB):
                            kcol = (hf * NKB + kb) * 512
                            w = ()
                            if j == 0:
                                w = list(bws[kb])
                                if kb == 0:
                                    w += [qfm_w[0]] + self.B("Kfm").rd()
                            elif kb == 0:
                                w = [qfm_w[j]]
                            tk = P.op(
                                "T", "matmul", self.pb[banks[kb]][:, :],
                                qfm[:, j * 512 + q0:j * 512 + q0 + 128],
                                self.Kfm[:, j * S + kcol:j * S + kcol + 512],
                                start=(j == 0), stop=(j == NCH - 1),
                                waits=w,
                                inc="T" if j == NCH - 1 else None)
                            if j == NCH - 1:
                                self.pbf[banks[kb]].wrote(self.mark("T", tk))
                                tks.append(tk)
                    blp = self.B("d_lp")
                    bnp = self.B("d_nmp")
                    for kb in range(NKB):
                        kcol = (hf * NKB + kb) * 512
                        li = hf * NKB + kb
                        tex = P.op("S", "activation", Pbuf[:, kcol:kcol + 512],
                                   self.pb[banks[kb]][:, :], AF.Exp,
                                   bias=nmp[:, :], scale=SCALE,
                                   accum_out=lpart[:, li:li + 1],
                                   waits=[tks[kb]] + bnp.rd() + wP + wlp +
                                         self.pbf[banks[kb]].rd(),
                                   inc="S")
                        self.bank_read(banks[kb], self.mark("S", tex))
                        bP.wrote(self.mark("S", tex))
                        blp.wrote(self.mark("S", tex))
                        bnp.read(self.mark("S", tex))
                    exp_tok[hf] = P.tok("S")
                def vop(meth, *a, waits=(), **kw):
                    return P.op("V", meth, *a,
                                waits=list(waits) + [("@", "V", P.pos["V"] - 1,
                                                      ("V", P.count["V"]))],
                                inc="V", **kw)
                t_l = vop("reduce_sum", lall[:, :], lpart[:, 0:2 * NKB], AX.X,
                          waits=[exp_tok[0], exp_tok[1]])
                self.B("d_lp").read(self.mark("V", t_l))
                trec = vop("reciprocal", lall[:, :], lall[:, :])
                tln = P.op("S", "activation", nmh[1][:, :], lall[:, :],
                           AF.Ln, waits=[self.mark("V", trec)] +
                           self.B("d_nmh1").begin(), inc="S")
                self.B("d_la").read(self.mark("S", tln))
                self.B("d_nmh1").wrote(self.mark("S", tln))
                tpr = vop("tensor_tensor", nmp[:, :], nmp[:, :],
                          nmh[1][:, :], ALU.add,
                          waits=[self.mark("S", tln)] +
                                self.B("d_nmp").begin())
                self.B("d_nmh1").read(self.mark("V", tpr))
                tpr = vop("tensor_scalar", nmp[:, :], nmp[:, :], 12.0, -12.0,
                          ALU.min, ALU.max)
                self.B("d_nmp").wrote(self.mark("V", tpr))
                LAG = 4
                tpv = [None, None]
                ogrp = [self.pbf[2].begin(), self.pbf[3].begin()]
                ptv_tok = {}
                for step in range(self.NKC + LAG):
                    kc = step - LAG
                    if kc >= 0:
                        slot = kc % 4
                        for h in range(2):
                            tpv[h] = P.op(
                                "T", "matmul", self.pb[2 + h][:, :],
                                PT[:, slot * 128:slot * 128 + 128],
                                self.Vs[:, kc * D + h * 512:kc * D + h * 512 + 512],
                                start=(kc == 0), stop=(kc == self.NKC - 1),
                                waits=([ptv_tok[kc]] + ogrp[h] +
                                       self.B("Vs").rd()) if kc == 0 else
                                      [ptv_tok[kc]],
                                inc="T" if kc == self.NKC - 1 else None)
                    kc_t = step
                    if kc_t < self.NKC:
                        bk = (0, 1, 4, 5)[kc_t % 4]
                        tmm = self.transpose(
                            bk, Pbuf[:, kc_t * 128:kc_t * 128 + 128],
                            waits=[exp_tok[0] if kc_t < self.NKC // 2
                                   else exp_tok[1]])
                        bP.read(self.mark("T", tmm))
                        slot = kc_t % 4
                        tptv = P.op("V", "tensor_copy",
                                    PT[:, slot * 128:slot * 128 + 128],
                                    self.pb[bk][:, 0:128],
                                    waits=self.pbf[bk].rd(),
                                    inc="V")
                        self.bank_read(bk, self.mark("V", tptv))
                        ptv_tok[kc_t] = self.mark("V", tptv)
                self.pbf[2].wrote(self.mark("T", tpv[0]))
                self.pbf[3].wrote(self.mark("T", tpv[1]))
                bO = self.B("d_O")
                wO = bO.begin()
                tevs = []
                for h in range(2):
                    tev = P.op("V", "tensor_scalar_mul",
                               Ot[:, h * 512:h * 512 + 512],
                               self.pb[2 + h][:, :], lall[:, :],
                               waits=self.pbf[2 + h].rd() + wO +
                                     [self.mark("V", trec)], inc="V")
                    self.bank_read(2 + h, self.mark("V", tev))
                    bO.wrote(self.mark("V", tev))
                    tevs.append(tev)
                bOT = self.B("d_OT")
                if s4 == 0:
                    self._wOT = bOT.begin()
                for c in range(NCH):
                    bk = c % 2
                    tmm = self.transpose(bk, Ot[:, c * 128:c * 128 + 128],
                                         waits=bO.rd())
                    bO.read(self.mark("T", tmm))
                    col = c * 512 + s4 * 128
                    tv = P.op("V", "tensor_copy", OT[:, col:col + 128],
                              self.pb[bk][:, 0:128],
                              waits=self.pbf[bk].rd() + self._wOT, inc="V")
                    self.bank_read(bk, self.mark("V", tv))
                    bOT.wrote(self.mark("V", tv))
                for h in range(2):
                    pat = h
                    bat = self.B(att_n[pat])
                    wat = bat.begin()
                    bank = 4 + h
                    twv = self.mm_group(
                        bank, self.pb[bank][:, :],
                        [(OT[:, c * 512 + s4 * 128:c * 512 + s4 * 128 + 128],
                          woTh[h][:, c * 512:(c + 1) * 512])
                         for c in range(NCH)],
                        waits=bOT.rd() + [tok_wo[h]])
                    bOT.read(self.mark("T", twv))
                    tve = P.op("V", "tensor_copy", att[pat][:, :],
                               self.pb[bank][:, :],
                               waits=self.pbf[bank].rd() + wat, inc="V")
                    self.bank_read(bank, self.mark("V", tve))
                    bat.wrote(self.mark("V", tve))
                    t0 = qt * 512 + s4 * 128
                    tsp = self.yd(
                        self.atsp_d[t0:t0 + 128, h * 512:h * 512 + 512],
                        att[pat][:, :], waits=bat.rd())
                    bat.read(tsp)
                    self._at_toks.append(tsp)
                kc = qt * 4 + s4
                pcm = kc % 2
                if len(self._cb_pend) >= 1:
                    emit_combT(*self._cb_pend.pop(0))
                lgs = lg4[:, s4 * NE:(s4 + 1) * NE]
                vop("reduce_max", r6[0][:, :], lgs, AX.X, negate=True)
                tnm2 = P.tok("V")
                tex2 = P.op("S", "activation", pe[:, :], lgs, AF.Exp,
                            bias=r6[0][:, :], accum_out=r6[1][:, :],
                            waits=[tnm2, self._rt_v_last], inc="S")
                self._rt_s_prev[s4] = self.mark("S", tex2)
                vop("reciprocal", r6[1][:, :], r6[1][:, :], waits=[tex2])
                vop("tensor_scalar_mul", pe[:, :], pe[:, :], r6[1][:, :])
                vop("reduce_max", r6[2][:, :], pe[:, :], AX.X)
                vop("tensor_scalar", msk[:, :], pe[:, :], r6[2][:, :], None,
                    ALU.is_lt)
                vop("tensor_mul", msk[:, :], msk[:, :], pe[:, :])
                vop("reduce_max", r6[3][:, :], msk[:, :], AX.X)
                vop("tensor_scalar", r6[4][:, :], r6[2][:, :], r6[3][:, :],
                    EPS_RT, ALU.add, ALU.add)
                vop("reciprocal", r6[4][:, :], r6[4][:, :])
                vop("tensor_scalar", r6[0][:, :], r6[2][:, :], r6[4][:, :],
                    None, ALU.mult)
                vop("tensor_scalar", r6[1][:, :], r6[3][:, :], r6[4][:, :],
                    None, ALU.mult)
                vop("tensor_scalar", eq1[:, :], pe[:, :], r6[2][:, :],
                    r6[0][:, :], ALU.is_ge, ALU.mult)
                vop("tensor_scalar", msk[:, :], msk[:, :], r6[3][:, :],
                    r6[1][:, :], ALU.is_ge, ALU.mult)
                bcm = self.B("d_comb%d" % pcm)
                tcb = vop("tensor_add", comb[pcm][:, :], eq1[:, :],
                          msk[:, :], waits=bcm.begin())
                self._rt_v_last = tcb
                bcm.wrote(self.mark("V", tcb))
                self._cb_pend.append((kc, pcm))
        while self._cb_pend:
            emit_combT(*self._cb_pend.pop(0))
        spill_cbt(self.NTT - 1)
        self.tok_atsp = list(self._at_toks)

    def phase_f(self):
        P, S = self.P, self.S
        self.new_phase()
        self.span.close()
        xnt_b = [self.sb("f_xnt%d" % i, [128, NCH * 512], BF16)
                 for i in range(2)]
        WGU = 2 * NCH * MID_R
        wgu = [self.sb("f_wgu%d" % i, [128, WGU], BF16) for i in range(2)]
        wg = [w[:, 0:NCH * MID_R] for w in wgu]
        wu = [w[:, NCH * MID_R:2 * NCH * MID_R] for w in wgu]
        wd = [self.sb("f_wd%d" % i, [128, 2 * D], BF16) for i in range(2)]
        wgs = self.sb("f_wgs", [128, NSH * NCH * MID_S], BF16)
        wus = self.sb("f_wus", [128, NSH * NCH * MID_S], BF16)
        wds = self.sb("f_wds", [128, NSH * 4 * D], BF16)
        cb = [self.sb("f_cb%d" % i, [128, 512], F32) for i in range(2)]
        sg = self.sb("f_sg", [128, 2 * 512], BF16)
        uu = self.sb("f_uu", [128, 2 * 512], BF16)
        sgs = self.sb("f_sgs", [128, 4 * 512], BF16)
        uus = self.sb("f_uus", [128, 4 * 512], BF16)
        hb = [self.sb("f_h%d" % e, [128, 2 * 512], BF16) for e in range(NE)]
        hs = [self.sb("f_hs%d" % n, [128, 4 * 512], BF16) for n in range(NSH)]
        onehot = self.sb("f_onehot", [16, NE * 128], BF16)
        combT = self.sb("f_combT", [16, S], BF16)
        x_sub = [self.sb("f_x%d" % i, [128, D], F32) for i in range(4)]
        at_sub = [self.sb("f_at%d" % i, [128, D], BF16) for i in range(4)]
        acc = [self.sb("f_acc%d" % i, [128, D], F32) for i in range(2)]

        def load_gu(e, g=False):
            bw = self.B("f_wgu%d" % (e % 2))
            w = self.tok_conv + bw.begin()
            if g:
                t = self.gload([(wgu[e % 2][:, :], self.wgu_bf[e])], waits=w)
            else:
                t = self.yd(wgu[e % 2][:, :], self.wgu_bf[e], waits=w)
            bw.wrote(t)
            return t

        tok_gu = {}
        tok_gu[0] = load_gu(0, g=True)
        tok_gu[1] = load_gu(1, g=True)
        t_oh = self.yd(onehot[:16, :], self.onehot_d)
        self.bufs["f_onehot"].wrote(t_oh)
        t_cbt = [self.yd(combT[:16, tt * 512:tt * 512 + 512],
                         self.cbsp_d[:16, tt * 512:tt * 512 + 512],
                         waits=[self.tok_cbsp[tt]])
                 for tt in range(self.NTT)]
        prs = []
        for n in range(NSH):
            prs.append((wgs[:, n * NCH * MID_S:(n + 1) * NCH * MID_S],
                        self.wgs_d[n]))
            prs.append((wus[:, n * NCH * MID_S:(n + 1) * NCH * MID_S],
                        self.wus_d[n]))
            prs.append((wds[:, n * 4 * D:(n + 1) * 4 * D], self.wds_d[n]))
        tok_ws = self.gload(prs)

        def load_wd(e):
            bwd = self.B("f_wd%d" % (e % 2))
            t = self.yd(wd[e % 2][:, :], self.wdr_bf[e],
                        waits=self.tok_conv + bwd.begin())
            bwd.wrote(t)
            return t

        tok_wd = {}

        def emit_cb(e, tt):
            bank = e % 2
            bf = self.pbf[bank]
            bcb = self.B("f_cb%d" % (e % 2))
            w0 = (bf.begin() + [t_cbt[tt]] +
                  self.B("f_onehot").rd())
            tcb = P.op("T", "matmul", self.pb[bank][:, :],
                       onehot[:16, e * 128:e * 128 + 128],
                       combT[:16, tt * 512:tt * 512 + 512],
                       start=True, stop=True, waits=w0, inc="T")
            bf.wrote(self.mark("T", tcb))
            tcbe = P.op("V", "tensor_copy", cb[e % 2][:, :],
                        self.pb[bank][:, :],
                        waits=bf.rd() + bcb.begin(), inc="V")
            self.bank_read(bank, self.mark("V", tcbe))
            bcb.wrote(self.mark("V", tcbe))

        for tt in range(self.NTT):
            if tt != 0:
                tok_gu[0], tok_gu[1] = self._gu_next
            xnt = xnt_b[tt % 2]
            bxt = self.B("f_xnt%d" % (tt % 2))
            if tt == 0:
                tldx = self.yd(xnt[:, :], self.xnsp_d[tt, :, :],
                               waits=self.tok_xnsp + bxt.begin())
                bxt.wrote(tldx)
            else:
                tldx = self._xn_next
            if tt + 1 < self.NTT:
                bxt2 = self.B("f_xnt%d" % ((tt + 1) % 2))
                self._xn_next = self.yd(xnt_b[(tt + 1) % 2][:, :],
                                        self.xnsp_d[tt + 1, :, :],
                                        waits=self.tok_xnsp + bxt2.begin())
                bxt2.wrote(self._xn_next)
            if tt == 0:
                emit_cb(0, tt)
                emit_cb(1, tt)
            for e in range(NE):
                pe_ = e % 2
                bcb = self.B("f_cb%d" % pe_)
                bhe = self.B("f_h%d" % e)
                bwg = self.B("f_wgu%d" % pe_)
                bwu = bwg
                whe = bhe.begin()
                for mc in range(2):
                    bg, bu = 2 + (2 * e) % 6, 2 + (2 * e + 1) % 6
                    tg = self.mm_group(
                        bg, self.pb[bg][:, :],
                        [(wg[pe_][:, c * MID_R + mc * 128:c * MID_R + mc * 128 + 128],
                          xnt[:, c * 512:(c + 1) * 512]) for c in range(NCH)],
                        waits=bxt.rd() + [tok_gu[e]])
                    bxt.read(self.mark("T", tg))
                    bwg.read(self.mark("T", tg))
                    tu = self.mm_group(
                        bu, self.pb[bu][:, :],
                        [(wu[pe_][:, c * MID_R + mc * 128:c * MID_R + mc * 128 + 128],
                          xnt[:, c * 512:(c + 1) * 512]) for c in range(NCH)],
                        waits=bxt.rd() + [tok_gu[e]])
                    bxt.read(self.mark("T", tu))
                    bwu.read(self.mark("T", tu))
                    bsg, buu = self.B("f_sg"), self.B("f_uu")
                    tsg = P.op("S", "activation", sg[:, mc * 512:mc * 512 + 512],
                               self.pb[bg][:, :], AF.Silu,
                               waits=self.pbf[bg].rd() +
                                     (bsg.begin() if mc == 0 else []), inc="S")
                    self.bank_read(bg, self.mark("S", tsg))
                    bsg.wrote(self.mark("S", tsg))
                    tuc = P.op("V", "scalar_tensor_tensor",
                               uu[:, mc * 512:mc * 512 + 512],
                               self.pb[bu][:, :], 1.0, cb[pe_][:, :],
                               ALU.mult, ALU.mult,
                               waits=self.pbf[bu].rd() + bcb.rd() +
                                     (buu.begin() if mc == 0 else []), inc="V")
                    self.bank_read(bu, self.mark("V", tuc))
                    bcb.read(self.mark("V", tuc))
                    buu.wrote(self.mark("V", tuc))
                    th = P.op("V", "tensor_mul",
                              hb[e][:, mc * 512:mc * 512 + 512],
                              sg[:, mc * 512:mc * 512 + 512],
                              uu[:, mc * 512:mc * 512 + 512],
                              waits=bsg.rd() + buu.rd() + whe +
                                    [self.mark("V", tuc)], inc="V")
                    bsg.read(self.mark("V", th))
                    buu.read(self.mark("V", th))
                    bhe.wrote(self.mark("V", th))
                if e + 2 < NE:
                    emit_cb(e + 2, tt)
                    tok_gu[e + 2] = load_gu(e + 2)
            for n in range(NSH):
                bhn = self.B("f_hs%d" % n)
                whn = bhn.begin()
                for mc in range(4):
                    bg = 2 + (2 * (NE + n)) % 6
                    bu = 2 + (2 * (NE + n) + 1) % 6
                    base = lambda c: (n * NCH + c) * MID_S + mc * 128
                    tg = self.mm_group(
                        bg, self.pb[bg][:, :],
                        [(wgs[:, base(c):base(c) + 128],
                          xnt[:, c * 512:(c + 1) * 512]) for c in range(NCH)],
                        waits=bxt.rd() + [tok_ws])
                    bxt.read(self.mark("T", tg))
                    tu = self.mm_group(
                        bu, self.pb[bu][:, :],
                        [(wus[:, base(c):base(c) + 128],
                          xnt[:, c * 512:(c + 1) * 512]) for c in range(NCH)],
                        waits=bxt.rd() + [tok_ws])
                    bxt.read(self.mark("T", tu))
                    bsg, buu = self.B("f_sgs"), self.B("f_uus")
                    tsg = P.op("S", "activation",
                               sgs[:, mc * 512:mc * 512 + 512],
                               self.pb[bg][:, :], AF.Silu,
                               waits=self.pbf[bg].rd() +
                                     (bsg.begin() if mc == 0 and n == 0 else []),
                               inc="S")
                    self.bank_read(bg, self.mark("S", tsg))
                    bsg.wrote(self.mark("S", tsg))
                    tuc = P.op("V", "tensor_copy",
                               uus[:, mc * 512:mc * 512 + 512],
                               self.pb[bu][:, :],
                               waits=self.pbf[bu].rd() +
                                     (buu.begin() if mc == 0 and n == 0 else []),
                               inc="V")
                    self.bank_read(bu, self.mark("V", tuc))
                    buu.wrote(self.mark("V", tuc))
                    th = P.op("V", "tensor_mul",
                              hs[n][:, mc * 512:mc * 512 + 512],
                              sgs[:, mc * 512:mc * 512 + 512],
                              uus[:, mc * 512:mc * 512 + 512],
                              waits=bsg.rd() + buu.rd() + whn +
                                    [self.mark("V", tuc)], inc="V")
                    bsg.read(self.mark("V", th))
                    buu.read(self.mark("V", th))
                    bhn.wrote(self.mark("V", th))
            if tt + 1 < self.NTT:
                emit_cb(0, tt + 1)
                emit_cb(1, tt + 1)
            ep_tok = {}
            for s4 in range(4):
                t0 = tt * 512 + s4 * 128
                bx, ba = self.B("f_x%d" % s4), self.B("f_at%d" % s4)
                tx = self.yd(x_sub[s4][:, :], self.x_d[t0:t0 + 128, :],
                             waits=bx.begin())
                bx.wrote(tx)
                ta = self.yd(at_sub[s4][:, :], self.atsp_d[t0:t0 + 128, :],
                             waits=self.tok_atsp + ba.begin())
                ba.wrote(ta)
                ep_tok[s4] = (tx, ta)
            if tt + 1 < self.NTT:
                self._gu_next = (load_gu(0), load_gu(1))
            h_rd = []
            for e in range(NE):
                h_rd += self.B("f_h%d" % e).rd()
            for n in range(NSH):
                h_rd += self.B("f_hs%d" % n).rd()
            grp_w = [self.pbf[b].begin() for b in range(8)]

            def down_routed():
                tok_wd[0] = load_wd(0)
                tok_wd[1] = load_wd(1)
                for e in range(NE):
                    for s4 in range(4):
                        for h in range(2):
                            b = s4 * 2 + h
                            for mc in range(2):
                                first = (e == 0 and mc == 0)
                                last_of_e = (s4 == 3 and h == 1 and mc == 1)
                                tok = P.op(
                                    "T", "matmul", self.pb[b][:, :],
                                    hb[e][:, mc * 512 + s4 * 128:mc * 512 + s4 * 128 + 128],
                                    wd[e % 2][:, mc * D + h * 512:mc * D + h * 512 + 512],
                                    start=first, stop=False,
                                    waits=(grp_w[b] + h_rd + [tok_wd[e]])
                                          if first else
                                          ([tok_wd[e]] if mc == 0 and
                                           s4 == 0 and h == 0 else ()),
                                    inc="T" if last_of_e else None)
                    self.B("f_wd%d" % (e % 2)).read(self.mark("T", tok))
                    if e + 2 < NE:
                        tok_wd[e + 2] = load_wd(e + 2)

            def down_shared(s4):
                for n in range(NSH):
                    for h in range(2):
                        b = s4 * 2 + h
                        for mc in range(4):
                            last = (n == NSH - 1 and mc == 3)
                            tok = P.op(
                                "T", "matmul", self.pb[b][:, :],
                                hs[n][:, mc * 512 + s4 * 128:mc * 512 + s4 * 128 + 128],
                                wds[:, (n * 4 + mc) * D + h * 512:(n * 4 + mc) * D + h * 512 + 512],
                                start=False, stop=last,
                                waits=[tok_ws] if n == 0 and mc == 0 and
                                      h == 0 else (),
                                inc="T" if last else None)
                            if last:
                                self.pbf[b].wrote(self.mark("T", tok))

            def epilogue(s4):
                t0 = tt * 512 + s4 * 128
                px = s4 % 2
                bx, ba, bac = (self.B("f_x%d" % s4), self.B("f_at%d" % s4),
                               self.B("f_acc%d" % px))
                tx, ta = ep_tok[s4]
                wac = bac.begin()
                for h in range(2):
                    b = s4 * 2 + h
                    tv1 = P.op("V", "tensor_add",
                               acc[px][:, h * 512:h * 512 + 512],
                               self.pb[b][:, :],
                               x_sub[s4][:, h * 512:h * 512 + 512],
                               waits=self.pbf[b].rd() + [tx] + wac, inc="V")
                    self.bank_read(b, self.mark("V", tv1))
                    bx.read(self.mark("V", tv1))
                    tv2 = P.op("V", "tensor_add",
                               acc[px][:, h * 512:h * 512 + 512],
                               acc[px][:, h * 512:h * 512 + 512],
                               at_sub[s4][:, h * 512:h * 512 + 512],
                               waits=[ta, self.mark("V", tv1)], inc="V")
                    ba.read(self.mark("V", tv2))
                    bac.wrote(self.mark("V", tv2))
                tst = self.yd(self.out_d[t0:t0 + 128, :], acc[px][:, :],
                              waits=bac.rd())
                bac.read(tst)

            down_routed()
            for s4 in (1, 2, 0, 3):
                down_shared(s4)
                epilogue(s4)
            tok_T = P.tok("T")
            for e in range(NE):
                self.B("f_h%d" % e).read(tok_T)
            for n in range(NSH):
                self.B("f_hs%d" % n).read(tok_T)
            bxt.read(tok_T)
        P.op("Y", "_wait_only", waits=[P.tok("dY%d" % i) for i in range(NYS)])

    def build(self):
        self.phase_b1()
        self.phase_b2()
        self.phase_d()
        self.phase_f()
        if self.sp is not None:
            self.sp.close()
        nc, P = self.nc, self.P
        sems = {}
        for s in SEMS:
            sems[s] = self.root.enter_context(nc.semaphore(s))
        with nc.Block() as block:
            @block.tensor
            def _(eng):
                P.replay("T", eng, sems)

            @block.vector
            def _(eng):
                P.replay("V", eng, sems)

            @block.scalar
            def _(eng):
                P.replay("S", eng, sems)

            @block.sync
            def _(eng):
                P.replay("Y", eng, sems)

            @block.gpsimd
            def _(eng):
                P.replay("G", eng, sems)
        self.root.close()
        return nc


def chunk_major(w):
    C = w.shape[0] // 128
    return np.ascontiguousarray(
        w.reshape(C, 128, w.shape[1]).transpose(1, 0, 2).reshape(128, -1))


def prep_weights(wq, wk, wv, wo, rw, wgr, wur, wdr, wgs, wus, wds):
    woT = np.asarray(wo, np.float32).T
    return {
        "wqT": chunk_major(np.ascontiguousarray(np.asarray(wq, np.float32).T))
                .reshape(128, NCH, NCH, 128).transpose(0, 2, 1, 3)
                .reshape(128, NCH * D).copy(),
        "wkT": chunk_major(np.ascontiguousarray(np.asarray(wk, np.float32).T)),
        "wvT": chunk_major(np.ascontiguousarray(np.asarray(wv, np.float32).T))
                .reshape(128, NCH, 2, 512).transpose(0, 2, 1, 3)
                .reshape(128, NCH * D).copy(),
        "woT": np.stack([chunk_major(np.ascontiguousarray(woT[:, h * 512:(h + 1) * 512]))
                         for h in range(2)]),
        "rw": chunk_major(np.asarray(rw, np.float32)),
        "wgu": np.stack([np.concatenate(
            [chunk_major(np.asarray(wgr[e], np.float32)),
             chunk_major(np.asarray(wur[e], np.float32))], axis=1)
            for e in range(NE)]),
        "wdr": np.stack([chunk_major(np.asarray(wdr[e], np.float32))
                         for e in range(NE)]),
        "wgs": np.stack([chunk_major(np.asarray(wgs[n], np.float32))
                         for n in range(NSH)]),
        "wus": np.stack([chunk_major(np.asarray(wus[n], np.float32))
                         for n in range(NSH)]),
        "wds": np.stack([chunk_major(np.asarray(wds[n], np.float32))
                         for n in range(NSH)]),
    }


def host_constants(S):
    NTT = S // 512
    ident = np.eye(128, dtype=np.float32)
    pswap = np.zeros((128, 128), dtype=np.float32)
    for i in range(64):
        pswap[2 * i, 2 * i + 1] = 1.0
        pswap[2 * i + 1, 2 * i] = 1.0
    onehot = np.zeros((16, NE * 128), dtype=np.float32)
    for e in range(NE):
        onehot[e, e * 128:(e + 1) * 128] = 1.0
    inv = 1.0 / (10000.0 ** (np.arange(0, D, 2, dtype=np.float32) / D))
    t = np.arange(S, dtype=np.float32)
    ang = np.outer(t, inv)
    cos = np.cos(ang).astype(np.float32)
    sin = np.sin(ang).astype(np.float32)
    Cfm = np.empty((D, S), dtype=np.float32)
    Sfm = np.empty((D, S), dtype=np.float32)
    Cfm[0::2, :] = cos.T
    Cfm[1::2, :] = cos.T
    Sfm[0::2, :] = -sin.T
    Sfm[1::2, :] = sin.T

    def tiles(X):
        out = np.empty((NTT, 128, NCH * 512), dtype=np.float32)
        for tt in range(NTT):
            for c in range(NCH):
                out[tt, :, c * 512:(c + 1) * 512] = \
                    X[c * 128:(c + 1) * 128, tt * 512:(tt + 1) * 512]
        return out

    cvals = np.zeros((128, 4), dtype=np.float32)
    cvals[:, 0] = 0.0
    cvals[:, 1] = 1.0
    cvals[:, 2] = EPS_LN
    cvals[:, 3] = SCALE
    return {
        "cvals": cvals,
        "ident": ident.astype(BFNP),
        "pswap": pswap.astype(BFNP),
        "onehot": onehot.astype(BFNP),
        "ctab": tiles(Cfm),
        "stab": tiles(Sfm),
    }


_NC_CACHE = {}


def get_nc(S=4096):
    if S not in _NC_CACHE:
        _NC_CACHE[S] = Builder(S).build()
    return _NC_CACHE[S]


def kernel(L0_attn_x, L0_attn_wq, L0_attn_wk, L0_attn_wv, L0_attn_wo,
           L0_ffn_router_w, L0_ffn_wg_r, L0_ffn_wu_r, L0_ffn_wd_r,
           L0_ffn_wg_s, L0_ffn_wu_s, L0_ffn_wd_s):
    S = 4096
    x = np.asarray(L0_attn_x, dtype=np.float32)
    base = prep_weights(L0_attn_wq, L0_attn_wk, L0_attn_wv, L0_attn_wo,
                        L0_ffn_router_w, L0_ffn_wg_r, L0_ffn_wu_r,
                        L0_ffn_wd_r, L0_ffn_wg_s, L0_ffn_wu_s, L0_ffn_wd_s)
    consts = host_constants(S)
    nc = get_nc(S)
    in_maps = []
    for i in range(8):
        m = {"x": np.ascontiguousarray(x[i])}
        m.update(base)
        m.update(consts)
        in_maps.append(m)
    res = run_bass_kernel_spmd(nc, in_maps, core_ids=list(range(8)))
    return np.stack([np.asarray(res.results[i]["out"]) for i in range(8)],
                    axis=0)


# version 48
# speedup vs baseline: 1.1370x; 1.0099x over previous
import math
from contextlib import ExitStack

import numpy as np
import ml_dtypes

import concourse.bass as bass
import concourse.mybir as mybir
from concourse.bass_utils import run_bass_kernel_spmd

F32 = mybir.dt.float32
BF16 = mybir.dt.bfloat16
AF = mybir.ActivationFunctionType
ALU = mybir.AluOpType
AX = mybir.AxisListType
BFNP = ml_dtypes.bfloat16

D = 1024
NCH = 8
NE = 16
MID_R = 256
MID_S = 512
NSH = 2
SCALE = 1.0 / 32.0
EPS_LN = 1e-5
EPS_RT = 1e-9

ENGS = ("T", "V", "S", "Y", "G")
NYS, NAS, NGS = 8, 4, 4
SEMS = (("T", "V", "S")
        + tuple("dY%d" % i for i in range(NYS))
        + tuple("dA%d" % i for i in range(NAS))
        + tuple("dG%d" % i for i in range(NGS)))


class Plan:

    def __init__(self):
        self.ops = {e: [] for e in ENGS}
        self.count = {s: 0 for s in SEMS}
        self.waited = {}
        self.pos = {e: 0 for e in ENGS}
        self.drained = {e: 0 for e in ENGS}

    def op(self, eng, meth, *args, waits=(), inc=None, inc_amt=1,
           dma=False, **kw):
        w = []
        drain = False
        for tok in waits:
            if tok is None:
                continue
            if tok[0] == "@":
                _, teng, tpos, stok = tok
                if teng == eng:
                    if eng == "T" or (eng == "S" and not dma):
                        continue
                    if stok is not None:
                        tok = stok
                    else:
                        if tpos >= self.drained[eng]:
                            drain = True
                            self.drained[eng] = self.pos[eng]
                        continue
                tok = stok
                if tok is None:
                    raise RuntimeError(
                        "cross-engine dep %s->%s without a semaphore token"
                        % (teng, eng))
            s, v = tok
            if v <= 0:
                continue
            key = (eng, s)
            if self.waited.get(key, 0) >= v:
                continue
            self.waited[key] = v
            w.append((s, v))
        incs = []
        tok = None
        if inc is not None:
            self.count[inc] += inc_amt
            tok = (inc, self.count[inc])
            incs.append((inc, inc_amt))
        self.ops[eng].append((meth, args, kw, tuple(w), tuple(incs), drain))
        self.pos[eng] += 1
        return tok

    def tok(self, s):
        return (s, self.count[s])

    def all_toks(self):
        return [(s, self.count[s]) for s in SEMS]

    def replay(self, eng_name, eng, sems):
        for meth, args, kw, waits, incs, drain in self.ops[eng_name]:
            if drain:
                eng.drain()
            for s, v in waits:
                eng.wait_ge(sems[s], v)
            if meth == "_wait_only":
                continue
            inst = getattr(eng, meth)(*args, **kw)
            for s, amt in incs:
                inst = inst.then_inc(sems[s], amt)


class Buf:

    def __init__(self):
        self.w = []
        self.r = []
        self.gen_waits = []

    def begin(self):
        self.gen_waits = self.w + self.r
        self.w = []
        self.r = []
        return list(self.gen_waits)

    def wrote(self, tok):
        if tok is not None:
            self.w.append(tok)

    def rd(self):
        return list(self.w)

    def read(self, tok):
        if tok is not None:
            self.r.append(tok)


class Builder:
    def __init__(self, S):
        assert S % 1024 == 0
        self.S = S
        self.NTT = S // 512
        self.NKC = S // 128
        self.NKB = S // 1024
        nc = bass.Bass("TRN2", target_bir_lowering=False, debug=False)
        self.nc = nc
        self.P = Plan()
        self.root = ExitStack()
        self.sp = None
        self.bufs = {}

        dr = lambda n, sh, dt, k=None: (
            nc.dram_tensor(n, sh, dt, kind=k) if k else nc.dram_tensor(n, sh, dt))
        self.x_d = dr("x", [S, D], F32, "ExternalInput").ap()
        self.wT_d = {w: dr(w, [128, NCH * D], F32, "ExternalInput").ap()
                     for w in ("wqT", "wkT", "wvT")}
        self.woT_d = dr("woT", [2, 128, NCH * 512], F32, "ExternalInput").ap()
        self.rw_d = dr("rw", [128, NCH * NE], F32, "ExternalInput").ap()
        self.wgu_d = dr("wgu", [NE, 128, 2 * NCH * MID_R], F32,
                        "ExternalInput").ap()
        self.wdr_d = dr("wdr", [NE, 128, 2 * D], F32, "ExternalInput").ap()
        self.wgs_d = dr("wgs", [NSH, 128, NCH * MID_S], F32, "ExternalInput").ap()
        self.wus_d = dr("wus", [NSH, 128, NCH * MID_S], F32, "ExternalInput").ap()
        self.wds_d = dr("wds", [NSH, 128, 4 * D], F32, "ExternalInput").ap()
        self.cvals_d = dr("cvals", [128, 4], F32, "ExternalInput").ap()
        self.ident_d = dr("ident", [128, 128], BF16, "ExternalInput").ap()
        self.pswap_d = dr("pswap", [128, 128], BF16, "ExternalInput").ap()
        self.onehot_d = dr("onehot", [16, NE * 128], BF16, "ExternalInput").ap()
        self.ctab_d = dr("ctab", [self.NTT, 128, NCH * 512], F32,
                         "ExternalInput").ap()
        self.stab_d = dr("stab", [self.NTT, 128, NCH * 512], F32,
                         "ExternalInput").ap()
        self.out_d = dr("out", [S, D], F32, "ExternalOutput").ap()
        self.xnsp_d = dr("xn_sp", [self.NTT, 128, NCH * 512], BF16).ap()
        self.wgu_bf = dr("wgu_bf", [NE, 128, 2 * NCH * MID_R], BF16).ap()
        self.wdr_bf = dr("wdr_bf", [NE, 128, 2 * D], BF16).ap()
        self.cbsp_d = dr("comb_sp", [16, S], BF16).ap()
        self.atsp_d = dr("attn_sp", [S, D], BF16).ap()

        self._g_rot = 0
        self._y_rot = 0
        self._a_rot = 0
        self.ident = self._sb(self.root, "identS", [128, 128], BF16)
        self.pswap = self._sb(self.root, "pswapS", [128, 128], BF16)
        self.cvals = self._sb(self.root, "cvalsS", [128, 4], F32)
        for j, v in enumerate((0.0, 1.0, EPS_LN, SCALE)):
            nc.const_aps.register(F32, v, self.cvals[:, j:j + 1])
        self.span = ExitStack()
        self.Kfm = self._sb(self.span, "Kfm", [128, NCH * S], BF16)
        self.Vs = self._sb(self.span, "Vs", [128, self.NKC * D], BF16)
        self.pb = [self.root.enter_context(
            nc.psum_tensor("pb%d" % i, [128, 512], F32)) for i in range(8)]
        self.pbf = [Buf() for _ in range(8)]

    def _sb(self, stack, name, shape, dt):
        t = stack.enter_context(self.nc.sbuf_tensor(name, shape, dt))
        self.bufs[name] = Buf()
        return t

    def new_phase(self):
        if self.sp is not None:
            self.sp.close()
        self.sp = ExitStack()
        toks = self.P.all_toks()
        for e in ENGS:
            self.P.op(e, "_wait_only", waits=toks)

    def sb(self, name, shape, dt):
        return self._sb(self.sp, name, shape, dt)

    def B(self, name):
        return self.bufs[name]

    def mark(self, eng, tok):
        return ("@", eng, self.P.pos[eng] - 1, tok)

    def gload(self, pairs, waits=()):
        s = "dG%d" % (self._g_rot % NGS)
        self._g_rot += 1
        tok = None
        first = True
        for dst, src in pairs:
            tok = self.P.op("G", "dma_start", dst, src,
                            waits=(list(waits) + [self.P.tok(s)])
                                  if first else (),
                            inc=s, inc_amt=16, dma=True)
            first = False
        return tok

    def yd(self, dst, src, waits=()):
        s = "dY%d" % (self._y_rot % NYS)
        self._y_rot += 1
        return self.P.op("Y", "dma_start", dst, src,
                         waits=list(waits) + [self.P.tok(s)],
                         inc=s, inc_amt=16, dma=True)

    def ad(self, dst, src, waits=()):
        s = "dA%d" % (self._a_rot % NAS)
        self._a_rot += 1
        return self.P.op("S", "dma_start", dst, src,
                         waits=list(waits) + [self.P.tok(s)],
                         inc=s, inc_amt=16, dma=True)

    def mm_group(self, bank, out_ap, lhsT_rhs_list, waits=()):
        bf = self.pbf[bank]
        w0 = bf.begin() + list(waits)
        n = len(lhsT_rhs_list)
        tok = None
        for i, (l, r) in enumerate(lhsT_rhs_list):
            tok = self.P.op("T", "matmul", out_ap, l, r,
                            start=(i == 0), stop=(i == n - 1),
                            waits=w0 if i == 0 else (),
                            inc="T" if i == n - 1 else None)
        bf.wrote(self.mark("T", tok))
        return tok

    def bank_read(self, bank, tok):
        self.pbf[bank].read(tok)

    def transpose(self, bank, src_ap, waits=(), out_parts=128):
        bf = self.pbf[bank]
        w0 = bf.begin() + list(waits) + self.B("identS").rd()
        tok = self.P.op("T", "matmul", self.pb[bank][:out_parts, 0:128],
                        src_ap, self.ident[:, :], start=True, stop=True,
                        waits=w0, inc="T")
        bf.wrote(self.mark("T", tok))
        return tok

    def phase_b1(self):
        P, S = self.P, self.S
        self.new_phase()
        wkT = self.sb("b_wkT", [128, NCH * D], BF16)
        x_sub = [self.sb("b_x%d" % i, [128, D], F32) for i in range(2)]
        sq = self.sb("b_sq", [128, D], BF16)
        stt = [self.sb("b_st%d" % i, [128, 1], F32) for i in range(6)]
        nbr_b = [self.sb("b_nbr%d" % i, [128, 2], F32) for i in range(2)]
        xn_tok_b = [self.sb("b_xntok%d" % i, [128, D], BF16)
                    for i in range(2)]
        xnt_b = [self.sb("b_xnt%d" % i, [128, NCH * 512], BF16)
                 for i in range(2)]
        kraw = [self.sb("b_kraw%d" % i, [128, 512], BF16) for i in range(2)]
        kswp = self.sb("b_kswp", [128, 512], BF16)
        kt1 = self.sb("b_kt1", [128, 512], BF16)
        ct = [self.sb("b_ct%d" % i, [128, 512], F32) for i in range(2)]
        st = [self.sb("b_stl%d" % i, [128, 512], F32) for i in range(2)]

        self._xnsp_toks = []
        tok_wk = self.gload([(wkT[:, :], self.wT_d["wkT"])])
        tcv = self.yd(self.cvals[:, :], self.cvals_d)
        self.P.op("S", "_wait_only", waits=[tcv])
        t1 = self.yd(self.ident[:, :], self.ident_d)
        t2 = self.yd(self.pswap[:, :], self.pswap_d)
        self.B("identS").wrote(t1)
        self.B("pswapS").wrote(t2)

        def emit_ln(tt, s4):
                t0 = tt * 512 + s4 * 128
                xb, bxb = x_sub[s4 % 2], self.B("b_x%d" % (s4 % 2))
                ldx = self.yd(xb[:, :], self.x_d[t0:t0 + 128, :],
                              waits=bxb.begin())
                bxb.wrote(ldx)
                bsq, bs0, bs1 = self.B("b_sq"), self.B("b_st0"), self.B("b_st1")
                bs2, bs4 = self.B("b_st2"), self.B("b_st4")
                bnb = self.B("b_nbr%d" % (s4 % 2))
                nbr = nbr_b[s4 % 2]
                tsq = P.op("S", "activation", sq[:, :], xb[:, :], AF.Square,
                           accum_out=stt[0][:, :],
                           waits=[ldx] + bsq.begin() + bs0.begin(), inc="S")
                bsq.wrote(self.mark("S", tsq))
                bs0.wrote(self.mark("S", tsq))
                tsm = P.op("V", "reduce_sum", stt[1][:, :], xb[:, :], AX.X,
                           waits=[ldx] + bs1.begin(), inc="V")
                bs1.wrote(self.mark("V", tsm))
                bxb.read(self.mark("V", tsm))
                bxb.read(self.mark("S", tsq))
                tmn = P.op("V", "tensor_scalar_mul", stt[2][:, :], stt[1][:, :],
                           1.0 / D, waits=bs2.begin() + bs1.rd(), inc="V")
                bs2.wrote(self.mark("V", tmn))
                tm2 = P.op("V", "tensor_mul", stt[3][:, :], stt[2][:, :],
                           stt[2][:, :],
                           waits=self.B("b_st3").begin() + bs2.rd(), inc="V")
                self.B("b_st3").wrote(self.mark("V", tm2))
                tv1 = P.op("V", "tensor_scalar", stt[4][:, :], stt[0][:, :],
                           1.0 / D, stt[3][:, :], ALU.mult, ALU.subtract,
                           waits=bs4.begin() + bs0.rd() +
                                 self.B("b_st3").rd(), inc="V")
                self.B("b_st3").read(self.mark("V", tv1))
                tv2 = P.op("V", "tensor_scalar_max", stt[4][:, :], stt[4][:, :],
                           0.0, waits=[self.mark("V", tv1)], inc="V")
                bs4.wrote(self.mark("V", tv2))
                trs = P.op("S", "activation", nbr[:, 1:2], stt[4][:, :],
                           AF.Sqrt, bias=EPS_LN,
                           waits=bnb.begin() + bs4.rd(), inc="S")
                bs4.read(self.mark("S", trs))
                bnb.wrote(self.mark("S", trs))
                trc = P.op("V", "reciprocal", nbr[:, 1:2], nbr[:, 1:2],
                           waits=[self.mark("S", trs)], inc="V")
                bnb.wrote(self.mark("V", trc))
                tnb = P.op("V", "tensor_scalar", nbr[:, 0:1], stt[2][:, :],
                           nbr[:, 1:2], -1.0, ALU.mult, ALU.mult,
                           waits=[self.mark("V", trc)] + bs2.rd(), inc="V")
                bs2.read(self.mark("V", tnb))
                bnb.wrote(self.mark("V", tnb))
                self._xn_w[s4 % 2] = [self.mark("V", tnb)]

        def emit_xn(tt, s4):
                p = s4 % 2
                xb, bxb = x_sub[p], self.B("b_x%d" % p)
                xn_tok, bxn = xn_tok_b[p], self.B("b_xntok%d" % p)
                bnb, nbr = self.B("b_nbr%d" % p), nbr_b[p]
                txn = P.op("V", "tensor_scalar", xn_tok[:, :], xb[:, :],
                           nbr[:, 1:2], nbr[:, 0:1], ALU.mult, ALU.add,
                           waits=bxn.begin() + self._xn_w[p], inc="V")
                bxn.wrote(self.mark("V", txn))
                bxb.read(self.mark("V", txn))
                bnb.read(self.mark("V", txn))

        def emit_trans(tt, grp):
            s4 = grp // 2
            xnt = xnt_b[tt % 2]
            bxt = self.B("b_xnt%d" % (tt % 2))
            xn_tok = xn_tok_b[s4 % 2]
            bxn = self.B("b_xntok%d" % (s4 % 2))
            for c in range((grp % 2) * 4, (grp % 2) * 4 + 4):
                bk = c % 2
                tmm = self.transpose(bk, xn_tok[:, c * 128:(c + 1) * 128],
                                     waits=bxn.rd() + self._wxt[tt % 2])
                bxn.read(self.mark("T", tmm))
                col = c * 512 + s4 * 128
                tev = P.op("S", "copy", xnt[:, col:col + 128],
                           self.pb[bk][:, 0:128],
                           waits=self.pbf[bk].rd(), inc="S")
                self.bank_read(bk, self.mark("S", tev))
                bxt.wrote(self.mark("S", tev))

        self._wxt = [None, None]
        self._xn_w = [None, None]
        for tt in range(self.NTT + 1):
            cur = tt if tt < self.NTT else None
            prv = tt - 1 if tt >= 1 else None
            if cur is not None:
                self._wxt[cur % 2] = self.B("b_xnt%d" % (cur % 2)).begin()
            pend = None
            tgrp = 0
            for j in range(NCH):
                if cur is not None and j % 2 == 0:
                    emit_ln(cur, j // 2)
                if prv is not None:
                    pcs = j % 2
                    xnt_p = xnt_b[prv % 2]
                    bxt_p = self.B("b_xnt%d" % (prv % 2))
                    bct, bst = self.B("b_ct%d" % pcs), self.B("b_stl%d" % pcs)
                    tc1 = self.yd(ct[pcs][:, :],
                                  self.ctab_d[prv, :, j * 512:(j + 1) * 512],
                                  waits=bct.begin())
                    bct.wrote(tc1)
                    tc2 = self.yd(st[pcs][:, :],
                                  self.stab_d[prv, :, j * 512:(j + 1) * 512],
                                  waits=bst.begin())
                    bst.wrote(tc2)
                    bank = (2, 3, 6, 7)[j % 4]
                    tproj = self.mm_group(
                        bank, self.pb[bank][:, :],
                        [(wkT[:, c * D + j * 128:c * D + j * 128 + 128],
                          xnt_p[:, c * 512:(c + 1) * 512]) for c in range(NCH)],
                        waits=bxt_p.rd() + [tok_wk])
                    bxt_p.read(self.mark("T", tproj))
                    if pend is not None:
                        self.rope_step(*pend, tbuf=(kt1, self.B("b_kt1")))
                    pend = (bank, kraw[j % 2], kswp, ct[pcs], st[pcs],
                            "b_kraw%d" % (j % 2), "b_kswp", "b_ct%d" % pcs,
                            "b_stl%d" % pcs, 4 + (j % 2),
                            self.Kfm, j * S + prv * 512, "Kfm")
                if cur is not None and j >= 2:
                    if j % 2 == 0:
                        emit_xn(cur, j // 2 - 1)
                    elif j == 7:
                        emit_xn(cur, 3)
                    emit_trans(cur, tgrp)
                    tgrp += 1
            if prv is not None:
                self.rope_step(*pend, tbuf=(kt1, self.B("b_kt1")))
            if cur is not None:
                while tgrp < 8:
                    emit_trans(cur, tgrp)
                    tgrp += 1
                xnt_c = xnt_b[cur % 2]
                bxt_c = self.B("b_xnt%d" % (cur % 2))
                tsp = self.ad(self.xnsp_d[cur, :, :], xnt_c[:, :],
                              waits=bxt_c.rd())
                bxt_c.read(tsp)
                self._xnsp_toks.append(tsp)
        self.tok_xnsp = [t for t in self._xnsp_toks]

    def rope_step(self, bank, raw, swp, ctile, stile, nraw, nswp, nct, nst,
                  swap_bank, dst, dcol, ndst, dst_w=(), tbuf=None):
        P = self.P
        braw, bswp = self.B(nraw), self.B(nswp)
        bct, bst = self.B(nct), self.B(nst)
        bdst = self.B(ndst)
        te = P.op("S", "copy", raw[:, :], self.pb[bank][:, :],
                  waits=self.pbf[bank].rd() + braw.begin(), inc="S")
        braw.wrote(self.mark("S", te))
        self.bank_read(bank, self.mark("S", te))
        sbf = self.pbf[swap_bank]
        w0 = sbf.begin() + braw.rd() + self.B("pswapS").rd()
        tm = P.op("T", "matmul", self.pb[swap_bank][:, :], self.pswap[:, :],
                  raw[:, :], start=True, stop=True, waits=w0, inc="T")
        sbf.wrote(self.mark("T", tm))
        braw.read(self.mark("T", tm))
        tv = P.op("V", "tensor_copy", swp[:, :], self.pb[swap_bank][:, :],
                  waits=sbf.rd() + bswp.begin(), inc="V")
        bswp.wrote(self.mark("V", tv))
        self.bank_read(swap_bank, self.mark("V", tv))
        dsl = dst[:, dcol:dcol + 512]
        if tbuf is not None:
            tb, btb = tbuf
            t1 = P.op("V", "tensor_mul", tb[:, :], raw[:, :], ctile[:, :],
                      waits=braw.rd() + bct.rd() + btb.begin(), inc="V")
            btb.wrote(self.mark("V", t1))
            braw.read(self.mark("V", t1))
            bct.read(self.mark("V", t1))
            t2 = P.op("V", "tensor_mul", swp[:, :], swp[:, :], stile[:, :],
                      waits=bst.rd() + [self.mark("V", tv)], inc="V")
            bst.read(self.mark("V", t2))
            t3 = P.op("V", "tensor_add", dsl, tb[:, :], swp[:, :],
                      waits=[self.mark("V", t1), self.mark("V", t2)] +
                            list(dst_w), inc="V")
            btb.read(self.mark("V", t3))
        else:
            t1 = P.op("V", "tensor_mul", dsl, raw[:, :], ctile[:, :],
                      waits=braw.rd() + bct.rd() + list(dst_w), inc="V")
            braw.read(self.mark("V", t1))
            bct.read(self.mark("V", t1))
            t2 = P.op("V", "tensor_mul", swp[:, :], swp[:, :], stile[:, :],
                      waits=bst.rd() + [self.mark("V", tv)], inc="V")
            bst.read(self.mark("V", t2))
            t3 = P.op("V", "tensor_add", dsl, dsl, swp[:, :],
                      waits=[self.mark("V", t1), self.mark("V", t2)],
                      inc="V")
        bswp.read(self.mark("V", t3))
        tok = self.mark("V", t3)
        bdst.wrote(tok)
        return tok

    def phase_b2(self):
        P, S = self.P, self.S
        self.new_phase()
        wvT = self.sb("c_wvT", [128, NCH * D], BF16)
        rws = self.sb("c_rws", [128, NCH * NE], BF16)
        xnt_b = [self.sb("c_xnt%d" % i, [128, NCH * 512], BF16)
                 for i in range(2)]
        lg = self.sb("c_lg", [128, NE], F32)
        pe = self.sb("c_pe", [128, NE], F32)
        msk = self.sb("c_msk", [128, NE], F32)
        eq1 = self.sb("c_eq1", [128, NE], F32)
        comb = [self.sb("c_comb%d" % i, [128, NE], BF16) for i in range(2)]
        r6 = [self.sb("c_r%d" % i, [128, 1], F32) for i in range(5)]
        combT = self.sb("c_combT", [16, S], BF16)
        self.bufs["combT"] = self.bufs["c_combT"]

        tok_wv = self.gload([(wvT[:, :], self.wT_d["wvT"]),
                             (rws[:, :], self.rw_d)])
        for e in range(NE):
            self.gload([(self.wgu_bf[e], self.wgu_d[e])])
            self.gload([(self.wdr_bf[e], self.wdr_d[e])])
        self.tok_conv = [self.P.tok("dG%d" % i) for i in range(NGS)]
        self._rt_v_last = None
        self._cb_pend = []

        def _emit_combT(kc, pcm, combT_=combT):
            bcm = self.B("c_comb%d" % pcm)
            tct = self.transpose(3, comb[pcm][:, :], waits=bcm.rd(),
                                 out_parts=16)
            bcm.read(self.mark("T", tct))
            tcv = P.op("V", "tensor_copy",
                       combT_[:16, kc * 128:kc * 128 + 128],
                       self.pb[3][:16, 0:128], waits=self.pbf[3].rd(),
                       inc="V")
            self.bank_read(3, self.mark("V", tcv))
            self.B("combT").wrote(self.mark("V", tcv))
        self._emit_combT = _emit_combT

        for tt in range(self.NTT):
            xnt = xnt_b[tt % 2]
            bxt = self.B("c_xnt%d" % (tt % 2))
            if tt == 0:
                tldx = self.yd(xnt[:, :], self.xnsp_d[tt, :, :],
                               waits=self.tok_xnsp + bxt.begin())
                bxt.wrote(tldx)
            else:
                tldx = self._cxn_next
            if tt + 1 < self.NTT:
                bxt2 = self.B("c_xnt%d" % ((tt + 1) % 2))
                self._cxn_next = self.yd(xnt_b[(tt + 1) % 2][:, :],
                                         self.xnsp_d[tt + 1, :, :],
                                         waits=self.tok_xnsp + bxt2.begin())
                bxt2.wrote(self._cxn_next)
            for s4 in range(4):
                kc = tt * 4 + s4
                pcm = kc % 2
                q0 = s4 * 128
                for h in range(2):
                    bank = (kc % 2) * 4 + h
                    tpv = self.mm_group(
                        bank, self.pb[bank][:, :],
                        [(xnt[:, c * 512 + q0:c * 512 + q0 + 128],
                          wvT[:, c * D + h * 512:c * D + h * 512 + 512])
                         for c in range(NCH)],
                        waits=bxt.rd() + [tok_wv])
                    bxt.read(self.mark("T", tpv))
                    tev = P.op("S", "copy",
                               self.Vs[:, kc * D + h * 512:kc * D + h * 512 + 512],
                               self.pb[bank][:, :], waits=self.pbf[bank].rd(),
                               inc="S")
                    self.bank_read(bank, self.mark("S", tev))
                    self.B("Vs").wrote(self.mark("S", tev))
                bank = 2
                trt = self.mm_group(
                    bank, self.pb[bank][:, 0:NE],
                    [(xnt[:, c * 512 + q0:c * 512 + q0 + 128],
                      rws[:, c * NE:(c + 1) * NE]) for c in range(NCH)],
                    waits=bxt.rd() + [tok_wv])
                bxt.read(self.mark("T", trt))
                if len(self._cb_pend) >= 2:
                    self._emit_combT(*self._cb_pend.pop(0))
                blg = self.B("c_lg")
                tlg = P.op("V", "tensor_copy", lg[:, :], self.pb[bank][:, 0:NE],
                           waits=self.pbf[bank].rd() + blg.begin(), inc="V")
                self.bank_read(bank, self.mark("V", tlg))
                blg.wrote(self.mark("V", tlg))
                def vop(meth, *a, waits=(), **kw):
                    return P.op("V", meth, *a,
                                waits=list(waits) + [("@", "V", P.pos["V"] - 1,
                                                      ("V", P.count["V"]))],
                                inc="V", **kw)
                vop("reduce_max", r6[0][:, :], lg[:, :], AX.X, negate=True)
                tnm = P.tok("V")
                tex = P.op("S", "activation", pe[:, :], lg[:, :], AF.Exp,
                           bias=r6[0][:, :], accum_out=r6[1][:, :],
                           waits=[tnm, self._rt_v_last], inc="S")
                blg.read(self.mark("S", tex))
                vop("reciprocal", r6[1][:, :], r6[1][:, :], waits=[tex])
                vop("tensor_scalar_mul", pe[:, :], pe[:, :], r6[1][:, :])
                vop("reduce_max", r6[2][:, :], pe[:, :], AX.X)
                vop("tensor_scalar", msk[:, :], pe[:, :], r6[2][:, :], None,
                    ALU.is_lt)
                vop("tensor_mul", msk[:, :], msk[:, :], pe[:, :])
                vop("reduce_max", r6[3][:, :], msk[:, :], AX.X)
                vop("tensor_scalar", r6[4][:, :], r6[2][:, :], r6[3][:, :],
                    EPS_RT, ALU.add, ALU.add)
                vop("reciprocal", r6[4][:, :], r6[4][:, :])
                vop("tensor_scalar", r6[0][:, :], r6[2][:, :], r6[4][:, :],
                    None, ALU.mult)
                vop("tensor_scalar", r6[1][:, :], r6[3][:, :], r6[4][:, :],
                    None, ALU.mult)
                vop("tensor_scalar", eq1[:, :], pe[:, :], r6[2][:, :],
                    r6[0][:, :], ALU.is_ge, ALU.mult)
                vop("tensor_scalar", msk[:, :], msk[:, :], r6[3][:, :],
                    r6[1][:, :], ALU.is_ge, ALU.mult)
                bcm = self.B("c_comb%d" % pcm)
                tcb = vop("tensor_add", comb[pcm][:, :], eq1[:, :], msk[:, :],
                          waits=bcm.begin())
                self._rt_v_last = tcb
                bcm.wrote(self.mark("V", tcb))
                self._cb_pend.append((kc, pcm, combT))
        while self._cb_pend:
            self._emit_combT(*self._cb_pend.pop(0))
        self.tok_cbsp = self.yd(self.cbsp_d, combT[:16, :],
                                waits=self.B("combT").rd())

    def phase_d(self):
        P, S = self.P, self.S
        NKB = self.NKB
        self.new_phase()
        wqT = self.sb("d_wqT", [128, NCH * D], BF16)
        woTh = [self.sb("d_woTh%d" % h, [128, NCH * 512], BF16)
                for h in range(2)]
        xnt = self.sb("d_xnt", [128, NCH * 512], BF16)
        qfm = self.sb("d_qfm", [128, NCH * 512], BF16)
        qraw = self.sb("d_qraw", [128, 512], BF16)
        qswp = self.sb("d_qswp", [128, 512], BF16)
        ct = [self.sb("d_ct%d" % i, [128, 512], F32) for i in range(2)]
        st = [self.sb("d_stl%d" % i, [128, 512], F32) for i in range(2)]
        Pb = [self.sb("d_P0", [128, S], BF16)]
        PT = self.sb("d_PT", [128, 4 * 128], BF16)
        Ot = self.sb("d_O", [128, D], BF16)
        OT = self.sb("d_OT", [128, NCH * 512], BF16)
        att = [self.sb("d_at%d" % i, [128, 512], BF16) for i in range(2)]
        nmh = [self.sb("d_nmh%d" % i, [128, 1], F32) for i in range(2)]
        lpart = self.sb("d_lp", [128, 8], F32)
        lall = self.sb("d_la", [128, 1], F32)
        nmp = self.sb("d_nmp", [128, 1], F32)

        self._at_toks = []
        tok_wq = self.gload([(wqT[:, :], self.wT_d["wqT"])])
        tok_wo = [self.gload([(woTh[h][:, :], self.woT_d[h])])
                  for h in range(2)]
        t0m = P.op("V", "memset", nmp[:, :], 0.0, inc="V")
        self.B("d_nmp").wrote(self.mark("V", t0m))

        for qt in range(self.NTT):
            bxt = self.B("d_xnt")
            if qt == 0:
                tldx = self.yd(xnt[:, :], self.xnsp_d[qt, :, :],
                               waits=self.tok_xnsp + bxt.begin())
                bxt.wrote(tldx)
            else:
                tldx = self._dxn_next
            bqf = self.B("d_qfm")
            wq0 = bqf.begin()
            qfm_w = [None] * NCH
            pend = None
            for j in range(NCH):
                pcs = j % 2
                bct, bst = self.B("d_ct%d" % pcs), self.B("d_stl%d" % pcs)
                tc1 = self.yd(ct[pcs][:, :],
                              self.ctab_d[qt, :, j * 512:(j + 1) * 512],
                              waits=bct.begin())
                bct.wrote(tc1)
                tc2 = self.yd(st[pcs][:, :],
                              self.stab_d[qt, :, j * 512:(j + 1) * 512],
                              waits=bst.begin())
                bst.wrote(tc2)
                bank = (6, 7, 2, 3)[j % 4]
                tproj = self.mm_group(
                    bank, self.pb[bank][:, :],
                    [(wqT[:, c * D + j * 128:c * D + j * 128 + 128],
                      xnt[:, c * 512:(c + 1) * 512]) for c in range(NCH)],
                    waits=bxt.rd() + [tok_wq])
                bxt.read(self.mark("T", tproj))
                if pend is not None:
                    qfm_w[pend[0]] = self.rope_step(*pend[1:], dst_w=wq0)
                pend = (j, bank, qraw, qswp, ct[pcs], st[pcs],
                        "d_qraw", "d_qswp", "d_ct%d" % pcs,
                        "d_stl%d" % pcs, 4 + (j % 2),
                        qfm, j * 512, "d_qfm")
            qfm_w[pend[0]] = self.rope_step(*pend[1:], dst_w=wq0)
            if qt + 1 < self.NTT:
                self._dxn_next = self.yd(xnt[:, :], self.xnsp_d[qt + 1, :, :],
                                         waits=self.tok_xnsp + bxt.begin())
                bxt.wrote(self._dxn_next)
            for s4 in range(4):
                pp = 0
                Pbuf, bP = Pb[0], self.B("d_P0")
                wP = bP.begin()
                q0 = s4 * 128
                exp_tok = [None, None]
                wlp = self.B("d_lp").begin()
                for hf in range(2):
                    banks = [hf * 4 + kb for kb in range(NKB)]
                    bws = [self.pbf[banks[kb]].begin() for kb in range(NKB)]
                    tks = []
                    for j in range(NCH):
                        for kb in range(NKB):
                            kcol = (hf * NKB + kb) * 512
                            w = ()
                            if j == 0:
                                w = list(bws[kb])
                                if kb == 0:
                                    w += [qfm_w[0]] + self.B("Kfm").rd()
                            elif kb == 0:
                                w = [qfm_w[j]]
                            tk = P.op(
                                "T", "matmul", self.pb[banks[kb]][:, :],
                                qfm[:, j * 512 + q0:j * 512 + q0 + 128],
                                self.Kfm[:, j * S + kcol:j * S + kcol + 512],
                                start=(j == 0), stop=(j == NCH - 1),
                                waits=w,
                                inc="T" if j == NCH - 1 else None)
                            if j == NCH - 1:
                                self.pbf[banks[kb]].wrote(self.mark("T", tk))
                                tks.append(tk)
                    blp = self.B("d_lp")
                    bnp = self.B("d_nmp")
                    for kb in range(NKB):
                        kcol = (hf * NKB + kb) * 512
                        li = hf * NKB + kb
                        tex = P.op("S", "activation", Pbuf[:, kcol:kcol + 512],
                                   self.pb[banks[kb]][:, :], AF.Exp,
                                   bias=nmp[:, :], scale=SCALE,
                                   accum_out=lpart[:, li:li + 1],
                                   waits=[tks[kb]] + bnp.rd() + wP + wlp +
                                         self.pbf[banks[kb]].rd(),
                                   inc="S")
                        self.bank_read(banks[kb], self.mark("S", tex))
                        bP.wrote(self.mark("S", tex))
                        blp.wrote(self.mark("S", tex))
                        bnp.read(self.mark("S", tex))
                    exp_tok[hf] = P.tok("S")
                def vop(meth, *a, waits=(), **kw):
                    return P.op("V", meth, *a,
                                waits=list(waits) + [("@", "V", P.pos["V"] - 1,
                                                      ("V", P.count["V"]))],
                                inc="V", **kw)
                t_l = vop("reduce_sum", lall[:, :], lpart[:, 0:2 * NKB], AX.X,
                          waits=[exp_tok[0], exp_tok[1]])
                self.B("d_lp").read(self.mark("V", t_l))
                trec = vop("reciprocal", lall[:, :], lall[:, :])
                tln = P.op("S", "activation", nmh[1][:, :], lall[:, :],
                           AF.Ln, waits=[self.mark("V", trec)] +
                           self.B("d_nmh1").begin(), inc="S")
                self.B("d_la").read(self.mark("S", tln))
                self.B("d_nmh1").wrote(self.mark("S", tln))
                tpr = vop("tensor_tensor", nmp[:, :], nmp[:, :],
                          nmh[1][:, :], ALU.add,
                          waits=[self.mark("S", tln)] +
                                self.B("d_nmp").begin())
                self.B("d_nmh1").read(self.mark("V", tpr))
                tpr = vop("tensor_scalar", nmp[:, :], nmp[:, :], 12.0, -12.0,
                          ALU.min, ALU.max)
                self.B("d_nmp").wrote(self.mark("V", tpr))
                LAG = 4
                tpv = [None, None]
                ogrp = [self.pbf[2].begin(), self.pbf[3].begin()]
                ptv_tok = {}
                for step in range(self.NKC + LAG):
                    kc = step - LAG
                    if kc >= 0:
                        slot = kc % 4
                        for h in range(2):
                            tpv[h] = P.op(
                                "T", "matmul", self.pb[2 + h][:, :],
                                PT[:, slot * 128:slot * 128 + 128],
                                self.Vs[:, kc * D + h * 512:kc * D + h * 512 + 512],
                                start=(kc == 0), stop=(kc == self.NKC - 1),
                                waits=([ptv_tok[kc]] + ogrp[h] +
                                       self.B("Vs").rd()) if kc == 0 else
                                      [ptv_tok[kc]],
                                inc="T" if kc == self.NKC - 1 else None)
                    kc_t = step
                    if kc_t < self.NKC:
                        bk = (0, 1, 4, 5)[kc_t % 4]
                        tmm = self.transpose(
                            bk, Pbuf[:, kc_t * 128:kc_t * 128 + 128],
                            waits=[exp_tok[0] if kc_t < self.NKC // 2
                                   else exp_tok[1]])
                        bP.read(self.mark("T", tmm))
                        slot = kc_t % 4
                        tptv = P.op("V", "tensor_copy",
                                    PT[:, slot * 128:slot * 128 + 128],
                                    self.pb[bk][:, 0:128],
                                    waits=self.pbf[bk].rd(),
                                    inc="V")
                        self.bank_read(bk, self.mark("V", tptv))
                        ptv_tok[kc_t] = self.mark("V", tptv)
                self.pbf[2].wrote(self.mark("T", tpv[0]))
                self.pbf[3].wrote(self.mark("T", tpv[1]))
                bO = self.B("d_O")
                wO = bO.begin()
                tevs = []
                for h in range(2):
                    tev = P.op("V", "tensor_scalar_mul",
                               Ot[:, h * 512:h * 512 + 512],
                               self.pb[2 + h][:, :], lall[:, :],
                               waits=self.pbf[2 + h].rd() + wO +
                                     [self.mark("V", trec)], inc="V")
                    self.bank_read(2 + h, self.mark("V", tev))
                    bO.wrote(self.mark("V", tev))
                    tevs.append(tev)
                bOT = self.B("d_OT")
                if s4 == 0:
                    self._wOT = bOT.begin()
                for c in range(NCH):
                    bk = c % 2
                    tmm = self.transpose(bk, Ot[:, c * 128:c * 128 + 128],
                                         waits=bO.rd())
                    bO.read(self.mark("T", tmm))
                    col = c * 512 + s4 * 128
                    tv = P.op("V", "tensor_copy", OT[:, col:col + 128],
                              self.pb[bk][:, 0:128],
                              waits=self.pbf[bk].rd() + self._wOT, inc="V")
                    self.bank_read(bk, self.mark("V", tv))
                    bOT.wrote(self.mark("V", tv))
                for h in range(2):
                    pat = h
                    bat = self.B("d_at%d" % pat)
                    wat = bat.begin()
                    bank = 4 + h
                    twv = self.mm_group(
                        bank, self.pb[bank][:, :],
                        [(OT[:, c * 512 + s4 * 128:c * 512 + s4 * 128 + 128],
                          woTh[h][:, c * 512:(c + 1) * 512])
                         for c in range(NCH)],
                        waits=bOT.rd() + [tok_wo[h]])
                    bOT.read(self.mark("T", twv))
                    tve = P.op("V", "tensor_copy", att[pat][:, :],
                               self.pb[bank][:, :],
                               waits=self.pbf[bank].rd() + wat, inc="V")
                    self.bank_read(bank, self.mark("V", tve))
                    bat.wrote(self.mark("V", tve))
                    t0 = qt * 512 + s4 * 128
                    tsp = self.yd(
                        self.atsp_d[t0:t0 + 128, h * 512:h * 512 + 512],
                        att[pat][:, :], waits=bat.rd())
                    bat.read(tsp)
                    self._at_toks.append(tsp)
        self.tok_atsp = list(self._at_toks)

    def phase_f(self):
        P, S = self.P, self.S
        self.new_phase()
        self.span.close()
        xnt_b = [self.sb("f_xnt%d" % i, [128, NCH * 512], BF16)
                 for i in range(2)]
        WGU = 2 * NCH * MID_R
        wgu = [self.sb("f_wgu%d" % i, [128, WGU], BF16) for i in range(2)]
        wg = [w[:, 0:NCH * MID_R] for w in wgu]
        wu = [w[:, NCH * MID_R:2 * NCH * MID_R] for w in wgu]
        wd = [self.sb("f_wd%d" % i, [128, 2 * D], BF16) for i in range(2)]
        wgs = self.sb("f_wgs", [128, NSH * NCH * MID_S], BF16)
        wus = self.sb("f_wus", [128, NSH * NCH * MID_S], BF16)
        wds = self.sb("f_wds", [128, NSH * 4 * D], BF16)
        cb = [self.sb("f_cb%d" % i, [128, 512], F32) for i in range(2)]
        sg = self.sb("f_sg", [128, 2 * 512], BF16)
        uu = self.sb("f_uu", [128, 2 * 512], BF16)
        sgs = self.sb("f_sgs", [128, 4 * 512], BF16)
        uus = self.sb("f_uus", [128, 4 * 512], BF16)
        hb = [self.sb("f_h%d" % e, [128, 2 * 512], BF16) for e in range(NE)]
        hs = [self.sb("f_hs%d" % n, [128, 4 * 512], BF16) for n in range(NSH)]
        onehot = self.sb("f_onehot", [16, NE * 128], BF16)
        combT = self.sb("f_combT", [16, S], BF16)
        x_sub = [self.sb("f_x%d" % i, [128, D], F32) for i in range(4)]
        at_sub = [self.sb("f_at%d" % i, [128, D], BF16) for i in range(4)]
        acc = [self.sb("f_acc%d" % i, [128, D], F32) for i in range(2)]

        def load_gu(e, g=False):
            bw = self.B("f_wgu%d" % (e % 2))
            w = self.tok_conv + bw.begin()
            if g:
                t = self.gload([(wgu[e % 2][:, :], self.wgu_bf[e])], waits=w)
            else:
                t = self.yd(wgu[e % 2][:, :], self.wgu_bf[e], waits=w)
            bw.wrote(t)
            return t

        tok_gu = {}
        tok_gu[0] = load_gu(0, g=True)
        tok_gu[1] = load_gu(1, g=True)
        t_oh = self.yd(onehot[:16, :], self.onehot_d)
        self.bufs["f_onehot"].wrote(t_oh)
        t_cb = self.yd(combT[:16, :], self.cbsp_d, waits=[self.tok_cbsp])
        self.bufs["f_combT"].wrote(t_cb)
        prs = []
        for n in range(NSH):
            prs.append((wgs[:, n * NCH * MID_S:(n + 1) * NCH * MID_S],
                        self.wgs_d[n]))
            prs.append((wus[:, n * NCH * MID_S:(n + 1) * NCH * MID_S],
                        self.wus_d[n]))
            prs.append((wds[:, n * 4 * D:(n + 1) * 4 * D], self.wds_d[n]))
        tok_ws = self.gload(prs)

        def load_wd(e):
            bwd = self.B("f_wd%d" % (e % 2))
            t = self.yd(wd[e % 2][:, :], self.wdr_bf[e],
                        waits=self.tok_conv + bwd.begin())
            bwd.wrote(t)
            return t

        tok_wd = {}

        def emit_cb(e, tt):
            bank = e % 2
            bf = self.pbf[bank]
            bcb = self.B("f_cb%d" % (e % 2))
            w0 = (bf.begin() + self.B("f_combT").rd() +
                  self.B("f_onehot").rd())
            tcb = P.op("T", "matmul", self.pb[bank][:, :],
                       onehot[:16, e * 128:e * 128 + 128],
                       combT[:16, tt * 512:tt * 512 + 512],
                       start=True, stop=True, waits=w0, inc="T")
            bf.wrote(self.mark("T", tcb))
            tcbe = P.op("V", "tensor_copy", cb[e % 2][:, :],
                        self.pb[bank][:, :],
                        waits=bf.rd() + bcb.begin(), inc="V")
            self.bank_read(bank, self.mark("V", tcbe))
            bcb.wrote(self.mark("V", tcbe))

        for tt in range(self.NTT):
            if tt != 0:
                tok_gu[0], tok_gu[1] = self._gu_next
            xnt = xnt_b[tt % 2]
            bxt = self.B("f_xnt%d" % (tt % 2))
            if tt == 0:
                tldx = self.yd(xnt[:, :], self.xnsp_d[tt, :, :],
                               waits=self.tok_xnsp + bxt.begin())
                bxt.wrote(tldx)
            else:
                tldx = self._xn_next
            if tt + 1 < self.NTT:
                bxt2 = self.B("f_xnt%d" % ((tt + 1) % 2))
                self._xn_next = self.yd(xnt_b[(tt + 1) % 2][:, :],
                                        self.xnsp_d[tt + 1, :, :],
                                        waits=self.tok_xnsp + bxt2.begin())
                bxt2.wrote(self._xn_next)
            if tt == 0:
                emit_cb(0, tt)
                emit_cb(1, tt)
            for e in range(NE):
                pe_ = e % 2
                bcb = self.B("f_cb%d" % pe_)
                bhe = self.B("f_h%d" % e)
                bwg = self.B("f_wgu%d" % pe_)
                bwu = bwg
                whe = bhe.begin()
                for mc in range(2):
                    bg, bu = 2 + (2 * e) % 6, 2 + (2 * e + 1) % 6
                    tg = self.mm_group(
                        bg, self.pb[bg][:, :],
                        [(wg[pe_][:, c * MID_R + mc * 128:c * MID_R + mc * 128 + 128],
                          xnt[:, c * 512:(c + 1) * 512]) for c in range(NCH)],
                        waits=bxt.rd() + [tok_gu[e]])
                    bxt.read(self.mark("T", tg))
                    bwg.read(self.mark("T", tg))
                    tu = self.mm_group(
                        bu, self.pb[bu][:, :],
                        [(wu[pe_][:, c * MID_R + mc * 128:c * MID_R + mc * 128 + 128],
                          xnt[:, c * 512:(c + 1) * 512]) for c in range(NCH)],
                        waits=bxt.rd() + [tok_gu[e]])
                    bxt.read(self.mark("T", tu))
                    bwu.read(self.mark("T", tu))
                    bsg, buu = self.B("f_sg"), self.B("f_uu")
                    tsg = P.op("S", "activation", sg[:, mc * 512:mc * 512 + 512],
                               self.pb[bg][:, :], AF.Silu,
                               waits=self.pbf[bg].rd() +
                                     (bsg.begin() if mc == 0 else []), inc="S")
                    self.bank_read(bg, self.mark("S", tsg))
                    bsg.wrote(self.mark("S", tsg))
                    tuc = P.op("V", "scalar_tensor_tensor",
                               uu[:, mc * 512:mc * 512 + 512],
                               self.pb[bu][:, :], 1.0, cb[pe_][:, :],
                               ALU.mult, ALU.mult,
                               waits=self.pbf[bu].rd() + bcb.rd() +
                                     (buu.begin() if mc == 0 else []), inc="V")
                    self.bank_read(bu, self.mark("V", tuc))
                    bcb.read(self.mark("V", tuc))
                    buu.wrote(self.mark("V", tuc))
                    th = P.op("V", "tensor_mul",
                              hb[e][:, mc * 512:mc * 512 + 512],
                              sg[:, mc * 512:mc * 512 + 512],
                              uu[:, mc * 512:mc * 512 + 512],
                              waits=bsg.rd() + buu.rd() + whe +
                                    [self.mark("V", tuc)], inc="V")
                    bsg.read(self.mark("V", th))
                    buu.read(self.mark("V", th))
                    bhe.wrote(self.mark("V", th))
                if e + 2 < NE:
                    emit_cb(e + 2, tt)
                    tok_gu[e + 2] = load_gu(e + 2)
            for n in range(NSH):
                bhn = self.B("f_hs%d" % n)
                whn = bhn.begin()
                for mc in range(4):
                    bg = 2 + (2 * (NE + n)) % 6
                    bu = 2 + (2 * (NE + n) + 1) % 6
                    base = lambda c: (n * NCH + c) * MID_S + mc * 128
                    tg = self.mm_group(
                        bg, self.pb[bg][:, :],
                        [(wgs[:, base(c):base(c) + 128],
                          xnt[:, c * 512:(c + 1) * 512]) for c in range(NCH)],
                        waits=bxt.rd() + [tok_ws])
                    bxt.read(self.mark("T", tg))
                    tu = self.mm_group(
                        bu, self.pb[bu][:, :],
                        [(wus[:, base(c):base(c) + 128],
                          xnt[:, c * 512:(c + 1) * 512]) for c in range(NCH)],
                        waits=bxt.rd() + [tok_ws])
                    bxt.read(self.mark("T", tu))
                    bsg, buu = self.B("f_sgs"), self.B("f_uus")
                    tsg = P.op("S", "activation",
                               sgs[:, mc * 512:mc * 512 + 512],
                               self.pb[bg][:, :], AF.Silu,
                               waits=self.pbf[bg].rd() +
                                     (bsg.begin() if mc == 0 and n == 0 else []),
                               inc="S")
                    self.bank_read(bg, self.mark("S", tsg))
                    bsg.wrote(self.mark("S", tsg))
                    tuc = P.op("V", "tensor_copy",
                               uus[:, mc * 512:mc * 512 + 512],
                               self.pb[bu][:, :],
                               waits=self.pbf[bu].rd() +
                                     (buu.begin() if mc == 0 and n == 0 else []),
                               inc="V")
                    self.bank_read(bu, self.mark("V", tuc))
                    buu.wrote(self.mark("V", tuc))
                    th = P.op("V", "tensor_mul",
                              hs[n][:, mc * 512:mc * 512 + 512],
                              sgs[:, mc * 512:mc * 512 + 512],
                              uus[:, mc * 512:mc * 512 + 512],
                              waits=bsg.rd() + buu.rd() + whn +
                                    [self.mark("V", tuc)], inc="V")
                    bsg.read(self.mark("V", th))
                    buu.read(self.mark("V", th))
                    bhn.wrote(self.mark("V", th))
            if tt + 1 < self.NTT:
                emit_cb(0, tt + 1)
                emit_cb(1, tt + 1)
            ep_tok = {}
            for s4 in range(4):
                t0 = tt * 512 + s4 * 128
                bx, ba = self.B("f_x%d" % s4), self.B("f_at%d" % s4)
                tx = self.yd(x_sub[s4][:, :], self.x_d[t0:t0 + 128, :],
                             waits=bx.begin())
                bx.wrote(tx)
                ta = self.yd(at_sub[s4][:, :], self.atsp_d[t0:t0 + 128, :],
                             waits=self.tok_atsp + ba.begin())
                ba.wrote(ta)
                ep_tok[s4] = (tx, ta)
            if tt + 1 < self.NTT:
                self._gu_next = (load_gu(0), load_gu(1))
            h_rd = []
            for e in range(NE):
                h_rd += self.B("f_h%d" % e).rd()
            for n in range(NSH):
                h_rd += self.B("f_hs%d" % n).rd()
            grp_w = [self.pbf[b].begin() for b in range(8)]

            def down_routed():
                tok_wd[0] = load_wd(0)
                tok_wd[1] = load_wd(1)
                for e in range(NE):
                    for s4 in range(4):
                        for h in range(2):
                            b = s4 * 2 + h
                            for mc in range(2):
                                first = (e == 0 and mc == 0)
                                last_of_e = (s4 == 3 and h == 1 and mc == 1)
                                tok = P.op(
                                    "T", "matmul", self.pb[b][:, :],
                                    hb[e][:, mc * 512 + s4 * 128:mc * 512 + s4 * 128 + 128],
                                    wd[e % 2][:, mc * D + h * 512:mc * D + h * 512 + 512],
                                    start=first, stop=False,
                                    waits=(grp_w[b] + h_rd + [tok_wd[e]])
                                          if first else
                                          ([tok_wd[e]] if mc == 0 and
                                           s4 == 0 and h == 0 else ()),
                                    inc="T" if last_of_e else None)
                    self.B("f_wd%d" % (e % 2)).read(self.mark("T", tok))
                    if e + 2 < NE:
                        tok_wd[e + 2] = load_wd(e + 2)

            def down_shared(s4):
                for n in range(NSH):
                    for h in range(2):
                        b = s4 * 2 + h
                        for mc in range(4):
                            last = (n == NSH - 1 and mc == 3)
                            tok = P.op(
                                "T", "matmul", self.pb[b][:, :],
                                hs[n][:, mc * 512 + s4 * 128:mc * 512 + s4 * 128 + 128],
                                wds[:, (n * 4 + mc) * D + h * 512:(n * 4 + mc) * D + h * 512 + 512],
                                start=False, stop=last,
                                waits=[tok_ws] if n == 0 and mc == 0 and
                                      h == 0 else (),
                                inc="T" if last else None)
                            if last:
                                self.pbf[b].wrote(self.mark("T", tok))

            def epilogue(s4):
                t0 = tt * 512 + s4 * 128
                px = s4 % 2
                bx, ba, bac = (self.B("f_x%d" % s4), self.B("f_at%d" % s4),
                               self.B("f_acc%d" % px))
                tx, ta = ep_tok[s4]
                wac = bac.begin()
                for h in range(2):
                    b = s4 * 2 + h
                    tv1 = P.op("V", "tensor_add",
                               acc[px][:, h * 512:h * 512 + 512],
                               self.pb[b][:, :],
                               x_sub[s4][:, h * 512:h * 512 + 512],
                               waits=self.pbf[b].rd() + [tx] + wac, inc="V")
                    self.bank_read(b, self.mark("V", tv1))
                    bx.read(self.mark("V", tv1))
                    tv2 = P.op("V", "tensor_add",
                               acc[px][:, h * 512:h * 512 + 512],
                               acc[px][:, h * 512:h * 512 + 512],
                               at_sub[s4][:, h * 512:h * 512 + 512],
                               waits=[ta, self.mark("V", tv1)], inc="V")
                    ba.read(self.mark("V", tv2))
                    bac.wrote(self.mark("V", tv2))
                tst = self.yd(self.out_d[t0:t0 + 128, :], acc[px][:, :],
                              waits=bac.rd())
                bac.read(tst)

            down_routed()
            for s4 in (1, 2, 0, 3):
                down_shared(s4)
                epilogue(s4)
            tok_T = P.tok("T")
            for e in range(NE):
                self.B("f_h%d" % e).read(tok_T)
            for n in range(NSH):
                self.B("f_hs%d" % n).read(tok_T)
            bxt.read(tok_T)
        P.op("Y", "_wait_only", waits=[P.tok("dY%d" % i) for i in range(NYS)])

    def build(self):
        self.phase_b1()
        self.phase_b2()
        self.phase_d()
        self.phase_f()
        if self.sp is not None:
            self.sp.close()
        nc, P = self.nc, self.P
        sems = {}
        for s in SEMS:
            sems[s] = self.root.enter_context(nc.semaphore(s))
        with nc.Block() as block:
            @block.tensor
            def _(eng):
                P.replay("T", eng, sems)

            @block.vector
            def _(eng):
                P.replay("V", eng, sems)

            @block.scalar
            def _(eng):
                P.replay("S", eng, sems)

            @block.sync
            def _(eng):
                P.replay("Y", eng, sems)

            @block.gpsimd
            def _(eng):
                P.replay("G", eng, sems)
        self.root.close()
        return nc


def chunk_major(w):
    C = w.shape[0] // 128
    return np.ascontiguousarray(
        w.reshape(C, 128, w.shape[1]).transpose(1, 0, 2).reshape(128, -1))


def prep_weights(wq, wk, wv, wo, rw, wgr, wur, wdr, wgs, wus, wds):
    woT = np.asarray(wo, np.float32).T
    return {
        "wqT": chunk_major(np.ascontiguousarray(np.asarray(wq, np.float32).T)),
        "wkT": chunk_major(np.ascontiguousarray(np.asarray(wk, np.float32).T)),
        "wvT": chunk_major(np.ascontiguousarray(np.asarray(wv, np.float32).T)),
        "woT": np.stack([chunk_major(np.ascontiguousarray(woT[:, h * 512:(h + 1) * 512]))
                         for h in range(2)]),
        "rw": chunk_major(np.asarray(rw, np.float32)),
        "wgu": np.stack([np.concatenate(
            [chunk_major(np.asarray(wgr[e], np.float32)),
             chunk_major(np.asarray(wur[e], np.float32))], axis=1)
            for e in range(NE)]),
        "wdr": np.stack([chunk_major(np.asarray(wdr[e], np.float32))
                         for e in range(NE)]),
        "wgs": np.stack([chunk_major(np.asarray(wgs[n], np.float32))
                         for n in range(NSH)]),
        "wus": np.stack([chunk_major(np.asarray(wus[n], np.float32))
                         for n in range(NSH)]),
        "wds": np.stack([chunk_major(np.asarray(wds[n], np.float32))
                         for n in range(NSH)]),
    }


def host_constants(S):
    NTT = S // 512
    ident = np.eye(128, dtype=np.float32)
    pswap = np.zeros((128, 128), dtype=np.float32)
    for i in range(64):
        pswap[2 * i, 2 * i + 1] = 1.0
        pswap[2 * i + 1, 2 * i] = 1.0
    onehot = np.zeros((16, NE * 128), dtype=np.float32)
    for e in range(NE):
        onehot[e, e * 128:(e + 1) * 128] = 1.0
    inv = 1.0 / (10000.0 ** (np.arange(0, D, 2, dtype=np.float32) / D))
    t = np.arange(S, dtype=np.float32)
    ang = np.outer(t, inv)
    cos = np.cos(ang).astype(np.float32)
    sin = np.sin(ang).astype(np.float32)
    Cfm = np.empty((D, S), dtype=np.float32)
    Sfm = np.empty((D, S), dtype=np.float32)
    Cfm[0::2, :] = cos.T
    Cfm[1::2, :] = cos.T
    Sfm[0::2, :] = -sin.T
    Sfm[1::2, :] = sin.T

    def tiles(X):
        out = np.empty((NTT, 128, NCH * 512), dtype=np.float32)
        for tt in range(NTT):
            for c in range(NCH):
                out[tt, :, c * 512:(c + 1) * 512] = \
                    X[c * 128:(c + 1) * 128, tt * 512:(tt + 1) * 512]
        return out

    cvals = np.zeros((128, 4), dtype=np.float32)
    cvals[:, 0] = 0.0
    cvals[:, 1] = 1.0
    cvals[:, 2] = EPS_LN
    cvals[:, 3] = SCALE
    return {
        "cvals": cvals,
        "ident": ident.astype(BFNP),
        "pswap": pswap.astype(BFNP),
        "onehot": onehot.astype(BFNP),
        "ctab": tiles(Cfm),
        "stab": tiles(Sfm),
    }


_NC_CACHE = {}


def get_nc(S=4096):
    if S not in _NC_CACHE:
        _NC_CACHE[S] = Builder(S).build()
    return _NC_CACHE[S]


def kernel(L0_attn_x, L0_attn_wq, L0_attn_wk, L0_attn_wv, L0_attn_wo,
           L0_ffn_router_w, L0_ffn_wg_r, L0_ffn_wu_r, L0_ffn_wd_r,
           L0_ffn_wg_s, L0_ffn_wu_s, L0_ffn_wd_s):
    S = 4096
    x = np.asarray(L0_attn_x, dtype=np.float32)
    base = prep_weights(L0_attn_wq, L0_attn_wk, L0_attn_wv, L0_attn_wo,
                        L0_ffn_router_w, L0_ffn_wg_r, L0_ffn_wu_r,
                        L0_ffn_wd_r, L0_ffn_wg_s, L0_ffn_wu_s, L0_ffn_wd_s)
    consts = host_constants(S)
    nc = get_nc(S)
    in_maps = []
    for i in range(8):
        m = {"x": np.ascontiguousarray(x[i])}
        m.update(base)
        m.update(consts)
        in_maps.append(m)
    res = run_bass_kernel_spmd(nc, in_maps, core_ids=list(range(8)))
    return np.stack([np.asarray(res.results[i]["out"]) for i in range(8)],
                    axis=0)
